# version 17
# baseline (speedup 1.0000x reference)
import contextlib
import math
import numpy as np
import concourse.bass as bass
import concourse.mybir as mybir
from concourse.bass_utils import run_bass_kernel_spmd

F32 = mybir.dt.float32
BF16 = mybir.dt.bfloat16
FP16 = mybir.dt.float16
AF = mybir.ActivationFunctionType
ALU = mybir.AluOpType
AX = mybir.AxisListType

T = 4096
D = 1024
HID = 2048
NE = 8
NH = 16
DH = 64
WIN = 1024
CW = 512
NCH = T // CW
ND = D // 128
NHC = HID // 128
RING = 3
VST = 65
EPS = 1e-5
ALIBI_START = 2.0 ** (-2.0 ** -(math.log2(NH) - 3.0))
SLOPES = [ALIBI_START * ALIBI_START ** i for i in range(NH)]

ENGS = ("pe", "act", "dve", "pool", "sync")


class B:
    def __init__(self, nc):
        self.nc = nc
        self.ops = {e: [] for e in ENGS}
        self.es = contextlib.ExitStack()
        self.sems = {}
        for name, owner in (("pe", "pe"), ("act", "act"), ("dve", "dve"),
                            ("pool", "pool"), ("ld", None), ("hw", None),
                            ("ph", None)):
            h = self.es.enter_context(nc.semaphore(name))
            self.sems[name] = [h, 0, owner]

    def emit(self, eng, fn, waits=(), inc=None, force=False):
        ws = []
        for w in waits:
            if w is None:
                continue
            sem, val = w
            if self.sems[sem][2] == eng and not force:
                continue
            ws.append((sem, val))
        tok = None
        if inc is not None:
            s = self.sems[inc]
            s[1] += (16 if inc in ("ld", "hw") else 1)
            tok = (inc, s[1])
        self.ops[eng].append((ws, fn, inc))
        return tok

    def realize(self):
        hiw = {e: {} for e in ENGS}

        def run_engine(eng, obj):
            hw = hiw[eng]
            for ws, fn, inc in self.ops[eng]:
                for sem, val in ws:
                    if hw.get(sem, 0) >= val:
                        continue
                    hw[sem] = val
                    obj.wait_ge(self.sems[sem][0], val)
                ins = fn(obj)
                if inc is not None:
                    ins.then_inc(self.sems[inc][0],
                                 16 if inc in ("ld", "hw") else 1)

        with self.nc.Block() as blk:
            blk.tensor(lambda t: run_engine("pe", t))
            blk.scalar(lambda s: run_engine("act", s))
            blk.vector(lambda v: run_engine("dve", v))
            blk.gpsimd(lambda g: run_engine("pool", g))
            blk.sync(lambda s: run_engine("sync", s))


def build():
    nc = bass.Bass("TRN2", target_bir_lowering=False)
    xt = nc.dram_tensor("xt", [D, T], F32, kind="ExternalInput").ap()
    wq = nc.dram_tensor("wq", [D, D], F32, kind="ExternalInput").ap()
    wk = nc.dram_tensor("wk", [D, D], F32, kind="ExternalInput").ap()
    wv = nc.dram_tensor("wv", [D, D], F32, kind="ExternalInput").ap()
    wo = nc.dram_tensor("wo", [D, D], F32, kind="ExternalInput").ap()
    bq = nc.dram_tensor("bq", [1, D], F32, kind="ExternalInput").ap()
    bk = nc.dram_tensor("bk", [1, D], F32, kind="ExternalInput").ap()
    bv = nc.dram_tensor("bv", [1, D], F32, kind="ExternalInput").ap()
    bo = nc.dram_tensor("bo", [1, D], F32, kind="ExternalInput").ap()
    rwt = nc.dram_tensor("rwt", [D, NE], F32, kind="ExternalInput").ap()
    rbt = nc.dram_tensor("rbt", [1, NE], F32, kind="ExternalInput").ap()
    sw1 = nc.dram_tensor("sw1", [D, HID], F32, kind="ExternalInput").ap()
    sb1 = nc.dram_tensor("sb1", [1, HID], F32, kind="ExternalInput").ap()
    sw2 = nc.dram_tensor("sw2", [HID, D], F32, kind="ExternalInput").ap()
    sb2 = nc.dram_tensor("sb2", [1, D], F32, kind="ExternalInput").ap()
    rw1 = nc.dram_tensor("rw1", [NE, D, HID], F32, kind="ExternalInput").ap()
    rb1 = nc.dram_tensor("rb1", [1, NE * HID], F32, kind="ExternalInput").ap()
    rw2 = nc.dram_tensor("rw2", [NE, HID, D], F32, kind="ExternalInput").ap()
    rb2 = nc.dram_tensor("rb2", [1, NE * D], F32, kind="ExternalInput").ap()
    yt = nc.dram_tensor("yt", [D, T], F32, kind="ExternalOutput").ap()

    wq_bf = nc.dram_tensor("wq_bf", [D, D], BF16, kind="Internal").ap()
    wk_bf = nc.dram_tensor("wk_bf", [D, D], BF16, kind="Internal").ap()
    wv_bf = nc.dram_tensor("wv_bf", [D, D], BF16, kind="Internal").ap()
    wo_bf = nc.dram_tensor("wo_bf", [D, D], BF16, kind="Internal").ap()
    sw1_bf = nc.dram_tensor("sw1_bf", [D, HID], BF16, kind="Internal").ap()
    sw2_bf = nc.dram_tensor("sw2_bf", [HID, D], BF16, kind="Internal").ap()
    rw1_bf = nc.dram_tensor("rw1_bf", [NE, D, HID], BF16, kind="Internal").ap()
    rw2_bf = nc.dram_tensor("rw2_bf", [NE, HID, D], BF16, kind="Internal").ap()
    y1_dram = nc.dram_tensor("y1_dram", [D, T], BF16, kind="Internal").ap()
    A_dram = nc.dram_tensor("A_dram", [NH, 128, 1920], BF16, kind="Internal").ap()

    b = B(nc)
    es = b.es

    PSA = es.enter_context(nc.psum_tensor("PSA", [128, 1024], F32))
    PSB = es.enter_context(nc.psum_tensor("PSB", [128, 1024], F32))
    PP45 = [es.enter_context(nc.psum_tensor(f"P{i}", [128, 512], F32))
            for i in (4, 5)]
    P6 = es.enter_context(nc.psum_tensor("P6", [128, 512], F32))
    P7 = es.enter_context(nc.psum_tensor("P7", [128, 512], F32))
    sq_sb = es.enter_context(nc.sbuf_tensor("sq_sb", [128, ND * CW], BF16))
    tmp0t = es.enter_context(nc.sbuf_tensor("tmp0t", [128, 512], BF16))
    tmp1t = es.enter_context(nc.sbuf_tensor("tmp1t", [128, 512], BF16))
    tmp0 = tmp0t[:, :]
    tmp1 = tmp1t[:, :]
    tmp_alias = True
    rows = es.enter_context(nc.sbuf_tensor("rows", [1, 3200], F32))
    ones_row_bf = es.enter_context(nc.sbuf_tensor("ones_row_bf", [1, 512], BF16))
    ones_row_f32 = es.enter_context(nc.sbuf_tensor("ones_row_f32", [1, 128], F32))
    ones_col_bf = es.enter_context(nc.sbuf_tensor("ones_col_bf", [128, 1], BF16))
    eye_f32 = es.enter_context(nc.sbuf_tensor("eye_f32", [128, 128], F32))
    iot = es.enter_context(nc.sbuf_tensor("iot", [128, 128], F32))
    bq_sb = es.enter_context(nc.sbuf_tensor("bq_sb", [1, D], BF16))
    bk_sb = es.enter_context(nc.sbuf_tensor("bk_sb", [1, D], BF16))
    bv_sb = es.enter_context(nc.sbuf_tensor("bv_sb", [1, D], BF16))
    bo_sb = es.enter_context(nc.sbuf_tensor("bo_sb", [1, D], BF16))
    sb1c = es.enter_context(nc.sbuf_tensor("sb1c", [128, NHC], F32))
    rb1c = es.enter_context(nc.sbuf_tensor("rb1c", [128, NE * NHC], F32))
    dummy = es.enter_context(nc.sbuf_tensor("dmy_bar", [1, 8], F32))

    RO_X, RO_O = 0, 1024
    ROW_M, ROW_MSQ = 2048, 2560
    NQ, NK, KC, MM_ = 3072, 3088, 3136, 3152

    tcast = []
    for src, dst in ((wq, wq_bf), (wk, wk_bf), (wv, wv_bf), (wo, wo_bf),
                     (sw1, sw1_bf), (sw2, sw2_bf)):
        tcast.append(b.emit("pool", lambda g, s=src, d=dst: g.dma_start(d, s),
                            inc="ld"))
    t_rw1c, t_rw2c = [], []
    for e in range(NE):
        t_rw1c.append(b.emit("pool", lambda g, e=e: g.dma_start(rw1_bf[e], rw1[e]),
                             inc="ld"))
        t_rw2c.append(b.emit("pool", lambda g, e=e: g.dma_start(rw2_bf[e], rw2[e]),
                             inc="ld"))
    t_bias = {}
    for nm, srcT, dst in (("bq", bq, bq_sb), ("bk", bk, bk_sb),
                          ("bv", bv, bv_sb), ("bo", bo, bo_sb)):
        t_bias[nm] = b.emit("pool", lambda g, a=srcT, d=dst: g.dma_start(d[:], a),
                            inc="ld")

    b.emit("dve", lambda v: v.memset(ones_row_bf[:], 1.0))
    b.emit("dve", lambda v: v.memset(ones_row_f32[:], 1.0))
    b.emit("dve", lambda v: v.memset(ones_col_bf[:], 1.0))
    t_iot = b.emit("pool", lambda g: g.iota(
        iot[:], pattern=[[-1, 128]], base=0, channel_multiplier=1,
        allow_small_or_imprecise_dtypes=True), inc="pool")
    t_eye = b.emit("dve", lambda v: v.tensor_single_scalar(
        eye_f32[:], iot[:], 0.0, ALU.is_equal), waits=[t_iot], inc="dve")

    def bias_cols(flat_sb, out_sb, n, tok_ld, psum):
        toks = []
        for g0 in range(0, n, 4):
            gn = min(4, n - g0)
            for j in range(gn):
                idx = g0 + j
                toks.append(b.emit(
                    "pe",
                    lambda t, i=idx, j=j, s=(j == 0), e=(j == gn - 1):
                    t.matmul(psum[0:128, j:j + 1],
                             flat_sb[0:1, i * 128:(i + 1) * 128],
                             ones_row_f32[0:1, 0:1], start=s, stop=e),
                    waits=([tok_ld, toks[-1] if toks else None]
                           if j == 0 else []),
                    inc="pe" if j == gn - 1 else None))
            toks.append(b.emit(
                "dve",
                lambda v, g0=g0, gn=gn: v.tensor_copy(
                    out_sb[:, g0:g0 + gn], psum[0:128, 0:gn]),
                waits=[toks[-1]], inc="dve"))
        return toks[-1]

    pre = contextlib.ExitStack()
    sb1_sb = pre.enter_context(nc.sbuf_tensor("sb1_sb", [1, HID], F32))
    rb1_sb = pre.enter_context(nc.sbuf_tensor("rb1_sb", [1, NE * HID], F32))
    t_sb1f = b.emit("sync", lambda s: s.dma_start(sb1_sb[:], sb1), inc="hw")
    t_rb1f = b.emit("sync", lambda s: s.dma_start(rb1_sb[:], rb1), inc="hw")
    t_sb1c = bias_cols(sb1_sb, sb1c, NHC, t_sb1f, P6)
    t_rb1c = bias_cols(rb1_sb, rb1c, NE * NHC, t_rb1f, P7)
    pre.close()

    att = contextlib.ExitStack()
    wqT = att.enter_context(nc.sbuf_tensor("wqT", [128, ND * D], BF16))
    wkT = att.enter_context(nc.sbuf_tensor("wkT", [128, ND * D], BF16))
    wvT = att.enter_context(nc.sbuf_tensor("wvT", [128, ND * D], BF16))
    woT = att.enter_context(nc.sbuf_tensor("woT", [128, ND * D], BF16))
    kr = att.enter_context(nc.sbuf_tensor("kr", [128, ND * RING * CW], BF16))
    vr = att.enter_context(nc.sbuf_tensor("vr", [128, 4 * RING * NH * VST], BF16))
    x_sb = [att.enter_context(nc.sbuf_tensor(f"x{i}", [128, ND * CW], BF16))
            for i in range(2)]
    h_sb = att.enter_context(nc.sbuf_tensor("h_sb", [128, ND * CW], BF16))
    q_sb = att.enter_context(nc.sbuf_tensor("q_sb", [128, ND * CW], BF16))
    attn_sb = att.enter_context(nc.sbuf_tensor("attn_sb", [128, ND * CW], BF16))
    y1c = attn_sb
    prW = [att.enter_context(nc.sbuf_tensor(f"pr{i}", [128, 2048], BF16))
           for i in range(2)]
    a_sb = att.enter_context(nc.sbuf_tensor("a0", [128, 1920], BF16))
    d_sb = att.enter_context(nc.sbuf_tensor("d_sb", [128, 1920], BF16))
    a_buf = [a_sb, d_sb]
    negM = att.enter_context(nc.sbuf_tensor("negM", [128, NH], F32))
    avraw = sq_sb[0:64, 0:512]
    rr_sb = sq_sb[0:1, 1024:2048].bitcast(F32)
    rr_bf = sq_sb[0:1, 2048:2560]

    t_d0 = b.emit("pool", lambda g: g.iota(
        d_sb[:], pattern=[[-1, 1920]], base=384, channel_multiplier=1,
        allow_small_or_imprecise_dtypes=True), inc="pool")
    t_d1 = b.emit("pool", lambda g: g.affine_select(
        d_sb[:], d_sb[:], pattern=[[1, 1920]], base=-384,
        channel_multiplier=-1, compare_op=ALU.is_ge, fill=-60000.0),
        waits=[t_d0], inc="pool")
    t_d = b.emit("pool", lambda g: g.affine_select(
        d_sb[:], d_sb[:], pattern=[[-1, 1920]], base=384 + 1023,
        channel_multiplier=1, compare_op=ALU.is_ge, fill=-60000.0),
        waits=[t_d1], inc="pool")
    t_ast = None
    for h in range(NH):
        t_ae = b.emit("act", lambda s, h=h: s.activation(
            a_sb[:], d_sb[:], AF.Exp, scale=float(SLOPES[h])),
            waits=[t_d, t_ast], inc="act")
        t_ast = b.emit("sync", lambda s, h=h: s.dma_start(A_dram[h], a_sb[:]),
                       waits=[t_ae], inc="hw")
    t_abuild_done = t_ast

    t_wT = []
    for w_src, w_dst, tok in ((wq_bf, wqT, tcast[0]), (wk_bf, wkT, tcast[1]),
                              (wv_bf, wvT, tcast[2]), (wo_bf, woT, tcast[3])):
        for dc in range(ND):
            t_wT.append(b.emit(
                "sync",
                lambda s, src=w_src, dst=w_dst, dc=dc: s.dma_start(
                    dst[:, dc * D:(dc + 1) * D],
                    src[:, dc * 128:(dc + 1) * 128], transpose=True),
                waits=[tok, t_sb1c, t_rb1c], inc="hw"))
    t_wqT, t_wkT, t_wvT, t_woT = t_wT[7], t_wT[15], t_wT[23], t_wT[31]

    t_vones = b.emit("dve", lambda v: v.memset(
        vr.rearrange("p (sg e) -> p sg e", e=VST)[:, :, 64:65], 1.0),
        inc="dve")

    def ln_stats(src_ap, n_free, tok_src, ps_s, ps_q, sq_war, extra=()):
        t_sq = b.emit("act", lambda s: s.activation(
            sq_sb[:], src_ap(None), AF.Square),
            waits=[tok_src, sq_war] + list(extra), inc="act")
        tok = None
        for dc in range(ND):
            b.emit("pe", lambda t, dc=dc: t.matmul(
                ps_s[0:1, :], ones_col_bf[:], src_ap(dc),
                start=(dc == 0), stop=(dc == ND - 1)),
                waits=[tok_src] + list(extra) if dc == 0 else [],
                inc="pe" if dc == ND - 1 else None)
        for dc in range(ND):
            tok = b.emit("pe", lambda t, dc=dc: t.matmul(
                ps_q[0:1, :], ones_col_bf[:],
                sq_sb[:, dc * CW:(dc + 1) * CW],
                start=(dc == 0), stop=(dc == ND - 1)),
                waits=[t_sq] + list(extra) if dc == 0 else [],
                inc="pe" if dc == ND - 1 else None)
        return t_sq, tok

    ln_state = {"act": None}

    def ln_rows(ps_s, ps_q, tok_stats, ro):
        m = rows[0:1, ROW_M:ROW_M + 512]
        msq = rows[0:1, ROW_MSQ:ROW_MSQ + 512]
        rstd = rows[0:1, ro:ro + 512]
        a2 = rows[0:1, ro + 512:ro + 1024]
        b.emit("dve", lambda v: v.tensor_single_scalar(
            m, ps_s[0:1, :], 1.0 / D, ALU.mult),
            waits=[tok_stats, ln_state["act"]])
        b.emit("dve", lambda v: v.tensor_scalar(
            msq, ps_q[0:1, :], 1.0 / D, EPS, ALU.mult, ALU.add))
        b.emit("dve", lambda v: v.tensor_mul(rstd, m, m))
        t_var = b.emit("dve", lambda v: v.tensor_sub(msq, msq, rstd),
                       inc="dve")
        b.emit("act", lambda s: s.activation(rstd, msq, AF.Ln),
               waits=[t_var])
        t_ex = b.emit("act", lambda s: s.activation(
            rstd, rstd, AF.Exp, scale=-0.5), inc="act")
        ln_state["act"] = t_ex
        return b.emit("dve", lambda v: v.scalar_tensor_tensor(
            a2, m, -1.0, rstd, ALU.mult, ALU.mult), waits=[t_ex], inc="dve")

    def ln_bcast(ro, tok_rows, ps_r, ps_a):
        tr = b.emit("pe", lambda t: t.matmul(
            ps_r[:, :], ones_row_f32[0:1, 0:128], rows[0:1, ro:ro + 512],
            start=True, stop=True), waits=[tok_rows], inc="pe")
        ta = b.emit("pe", lambda t: t.matmul(
            ps_a[:, :], ones_row_f32[0:1, 0:128],
            rows[0:1, ro + 512:ro + 1024], start=True, stop=True),
            waits=[tok_rows], inc="pe")
        return tr, ta

    ps45 = {0: None, 1: None}
    uw = [None, None]
    t_prw_war = {0: None, 1: None}
    t_aload = {0: None, 1: None}
    t_a_war = None
    t_last_exp = None
    t_negM_evac = None
    t_recip_mult = None
    t_rcm = [None]
    t_attn_read = None
    t_krw = {}
    t_y1_pass3 = [None, None]
    t_y1_store = [None] * NCH
    t_osq_prev = None
    t_prev_statq = None

    for c in range(NCH):
        xb = x_sb[c % 2]
        slot = c % RING
        t_x = b.emit("pool", lambda g, c=c, xb=xb: g.dma_start(
            xb[:], xt.rearrange("(dc p) t -> p dc t", p=128)
            [:, :, c * CW:(c + 1) * CW]),
            waits=[t_y1_pass3[c % 2]], inc="ld")
        extra0 = [t_sb1c, t_rb1c] if c == 0 else [t_y1_pass3[(c - 1) % 2]]
        t_sqx, t_stx = ln_stats(
            lambda dc, xb=xb: xb[:] if dc is None else xb[:, dc * CW:(dc + 1) * CW],
            CW, t_x, P6, P7, t_prev_statq, extra=extra0)
        t_rowsx = ln_rows(P6, P7, t_stx, RO_X)
        t_R, t_A2 = ln_bcast(RO_X, t_rowsx, P6, P7)
        t_h = None
        for dc in range(ND):
            b.emit("dve", lambda v, dc=dc, xb=xb: v.tensor_mul(
                tmp0, xb[:, dc * CW:(dc + 1) * CW], P6[:, :]),
                waits=[t_R, t_A2, t_osq_prev] if dc == 0 else [])
            t_h = b.emit("dve", lambda v, dc=dc: v.tensor_add(
                h_sb[:, dc * CW:(dc + 1) * CW], tmp0, P7[:, :]),
                inc="dve" if dc == ND - 1 else None)

        def proj_fm(wT_sb, t_w, bias_row, out_cb):
            toks = []
            for dco in range(ND):
                pi = dco % 2
                ps = PP45[pi]
                for dci in range(ND):
                    b.emit("pe", lambda t, dco=dco, dci=dci, ps=ps: t.matmul(
                        ps[:, :],
                        wT_sb[:, dci * D + dco * 128:dci * D + (dco + 1) * 128],
                        h_sb[:, dci * CW:(dci + 1) * CW],
                        start=(dci == 0), stop=False),
                        waits=[t_h, t_w, ps45[pi]] if dci == 0 else [])
                tm = b.emit("pe", lambda t, dco=dco, ps=ps: t.matmul(
                    ps[:, :], bias_row[0:1, dco * 128:(dco + 1) * 128],
                    ones_row_bf[0:1, 0:512], start=False, stop=True),
                    inc="pe")
                last = b.emit("dve", lambda v, dco=dco, ps=ps: v.tensor_copy(
                    out_cb(dco), ps[:, :]), waits=[tm], inc="dve")
                ps45[pi] = last
                toks.append(last)
            return toks

        q_toks = proj_fm(wqT, t_wqT, bq_sb,
                         lambda dco: q_sb[:, dco * CW:(dco + 1) * CW])
        t_q = q_toks[-1]
        q_sq = []
        for dcq in range(ND):
            q_sq.append(b.emit("act", lambda s, dcq=dcq: s.activation(
                sq_sb[:, dcq * CW:(dcq + 1) * CW],
                q_sb[:, dcq * CW:(dcq + 1) * CW], AF.Square),
                waits=[q_toks[dcq]] +
                ([t_prev_statq, t_stx] if dcq == 0 else []),
                inc="act"))
        k_toks = proj_fm(wkT, t_wkT, bk_sb,
                         lambda dco, slot=slot: kr[:, dco * RING * CW + slot * CW:
                                                   dco * RING * CW + (slot + 1) * CW])
        t_k = k_toks[-1]
        t_krw[slot] = t_k
        t_v = None
        for tt in range(4):
            for fh in range(2):
                ps = PP45[fh]
                for dci in range(ND):
                    b.emit("pe", lambda t, tt=tt, fh=fh, dci=dci, ps=ps:
                           t.matmul(ps[:, :],
                                    h_sb[:, dci * CW + tt * 128:
                                         dci * CW + (tt + 1) * 128],
                                    wvT[:, dci * D + fh * 512:
                                        dci * D + (fh + 1) * 512],
                                    start=(dci == 0), stop=False),
                           waits=[t_h, t_wvT, t_vones, ps45[fh]]
                           if dci == 0 else [])
                tm = b.emit("pe", lambda t, fh=fh, ps=ps: t.matmul(
                    ps[:, :], ones_row_bf[0:1, 0:128],
                    bv_sb[0:1, fh * 512:(fh + 1) * 512],
                    start=False, stop=True), inc="pe")
                vslot = (slot * 4 + tt) * NH * VST
                t_v = b.emit("dve", lambda v, fh=fh, ps=ps, vs=vslot:
                             v.tensor_copy(
                                 vr[:, vs + fh * 8 * VST:vs + (fh + 1) * 8 * VST]
                                 .rearrange("p (g e) -> p g e", e=VST)
                                 [:, :, 0:64],
                                 ps.rearrange("p (g e) -> p g e", e=64)),
                             waits=[tm], inc="dve")
                ps45[fh] = t_v

        nslots = [PSA[0:1, 0:512], PSA[0:1, 512:1024],
                  PSB[0:1, 0:512], PSB[0:1, 512:1024]]
        ntoks = [None] * 4
        t_qn_done = [None] * ND
        for p_ in range(NH // 2):
            s0, s1 = (2 * p_) % 4, (2 * p_) % 4 + 1
            tq = b.emit("pe", lambda t, dc=p_, ns=nslots[s0]: t.matmul(
                ns, ones_col_bf[0:64, :],
                sq_sb[0:64, dc * CW:(dc + 1) * CW],
                start=True, stop=True),
                waits=[q_sq[p_], ntoks[s0], ntoks[s1]] +
                ([uw[0], uw[1]] if p_ == 0 else []))
            tq = b.emit("pe", lambda t, dc=p_, ns=nslots[s1]: t.matmul(
                ns, ones_col_bf[64:128, :],
                sq_sb[64:128, dc * CW:(dc + 1) * CW],
                start=True, stop=True), inc="pe")
            ntoks[s0] = b.emit(
                "dve", lambda v, h=2 * p_, ns=nslots[s0]: v.reduce_max(
                    rows[0:1, NQ + h:NQ + h + 1], ns, axis=AX.X),
                waits=[tq], inc="dve")
            ntoks[s1] = b.emit(
                "dve", lambda v, h=2 * p_ + 1, ns=nslots[s1]: v.reduce_max(
                    rows[0:1, NQ + h:NQ + h + 1], ns, axis=AX.X),
                inc="dve")
            t_qn_done[p_] = ntoks[s1]
        k_sq = []
        for dck in range(ND):
            kcolb = dck * RING * CW + slot * CW
            k_sq.append(b.emit("act", lambda s, dck=dck, kcolb=kcolb:
                               s.activation(
                                   sq_sb[:, dck * CW:(dck + 1) * CW],
                                   kr[:, kcolb:kcolb + CW], AF.Square),
                               waits=[k_toks[dck], q_sq[dck],
                                      t_qn_done[dck]], inc="act"))
        nrm = None
        for p_ in range(NH // 2):
            s0, s1 = (2 * p_) % 4, (2 * p_) % 4 + 1
            tq = b.emit("pe", lambda t, dc=p_, ns=nslots[s0]: t.matmul(
                ns, ones_col_bf[0:64, :],
                sq_sb[0:64, dc * CW:(dc + 1) * CW],
                start=True, stop=True),
                waits=[k_sq[p_], ntoks[s0], ntoks[s1]])
            tq = b.emit("pe", lambda t, dc=p_, ns=nslots[s1]: t.matmul(
                ns, ones_col_bf[64:128, :],
                sq_sb[64:128, dc * CW:(dc + 1) * CW],
                start=True, stop=True), inc="pe")
            ntoks[s0] = b.emit(
                "dve", lambda v, h=2 * p_, slot=slot, ns=nslots[s0]:
                v.reduce_max(
                    rows[0:1, NK + slot * 16 + h:NK + slot * 16 + h + 1],
                    ns, axis=AX.X), waits=[tq], inc="dve")
            nrm = ntoks[s1] = b.emit(
                "dve", lambda v, h=2 * p_ + 1, slot=slot, ns=nslots[s1]:
                v.reduce_max(
                    rows[0:1, NK + slot * 16 + h:NK + slot * 16 + h + 1],
                    ns, axis=AX.X), inc="dve")
        uw[0] = uw[1] = nrm
        slots_present = [s for s in range(RING) if s <= c]
        b.emit("dve", lambda v, s0=slots_present[0]: v.tensor_copy(
            rows[0:1, KC:KC + 16], rows[0:1, NK + s0 * 16:NK + s0 * 16 + 16]),
            waits=[nrm])
        for s in slots_present[1:]:
            b.emit("dve", lambda v, s=s: v.tensor_max(
                rows[0:1, KC:KC + 16], rows[0:1, KC:KC + 16],
                rows[0:1, NK + s * 16:NK + s * 16 + 16]))
        b.emit("dve", lambda v: v.tensor_mul(
            rows[0:1, MM_:MM_ + 16], rows[0:1, NQ:NQ + 16],
            rows[0:1, KC:KC + 16]))
        t_m1 = b.emit("dve", lambda v: v.tensor_single_scalar(
            rows[0:1, MM_:MM_ + 16], rows[0:1, MM_:MM_ + 16], 1e-30, ALU.max),
            inc="dve")
        b.emit("act", lambda s: s.activation(
            rows[0:1, MM_:MM_ + 16], rows[0:1, MM_:MM_ + 16], AF.Ln),
            waits=[t_m1])
        t_m2 = b.emit("act", lambda s: s.activation(
            rows[0:1, MM_:MM_ + 16], rows[0:1, MM_:MM_ + 16], AF.Exp,
            scale=0.5), inc="act")
        t_m3 = b.emit("dve", lambda v: v.tensor_single_scalar(
            rows[0:1, MM_:MM_ + 16], rows[0:1, MM_:MM_ + 16], -0.125,
            ALU.mult), waits=[t_m2], inc="dve")
        tmm = b.emit("pe", lambda t: t.matmul(
            P7[0:128, 0:NH], ones_row_f32[0:1, 0:128],
            rows[0:1, MM_:MM_ + 16], start=True, stop=True),
            waits=[t_m3, t_rcm[0]], inc="pe")
        t_negM_evac = b.emit("dve", lambda v: v.tensor_copy(
            negM[:], P7[0:128, 0:NH]), waits=[tmm, t_last_exp], inc="dve")

        kts = list(range(max(0, 4 * c - 8), 4 * c + 4))
        nkt = len(kts)
        nwav = nkt // 2
        if c == 0:
            t_aload[0] = b.emit("sync", lambda s: s.dma_start(
                a_buf[0][:], A_dram[0]), waits=[t_abuild_done], inc="hw")
            t_aload[1] = b.emit("sync", lambda s: s.dma_start(
                a_buf[1][:], A_dram[1]), waits=[t_abuild_done], inc="hw")
        pending_end = [None]
        for p in range(NH // 2):
            dc = p
            SPv = [PSA, PSB]
            t_avlast = [None, None]
            t_amlast = [None, None]
            pend = None

            def emit_avs(item, dc=dc):
                w, am_toks, vcols = item
                half = w % 2
                for hh in range(2):
                    for j in range(2):
                        idx = 2 * w + j
                        prq = prW[hh][:, half * 1024 + j * 512:
                                      half * 1024 + j * 512 + 512]
                        t_av = b.emit(
                            "pe", lambda t, vcol=vcols[2 * hh + j], prq=prq,
                            hh=hh, s=(idx == 0), e=(idx == nkt - 1): t.matmul(
                                PP45[hh][0:65, :], vr[:, vcol:vcol + VST],
                                prq, start=s, stop=e),
                            waits=[am_toks[2 * hh + j]] +
                            ([ps45[hh]] if idx == 0 else []),
                            inc="pe")
                        t_avlast[hh] = t_av
                    t_prw_war[hh] = t_avlast[hh]

            for w in range(nwav):
                half = w % 2
                mm_tok = [None, None]
                vcols = [0, 0, 0, 0]
                for hh in range(2):
                    pb = 64 * hh
                    for j in range(2):
                        kt = kts[2 * w + j]
                        cs = kt // 4
                        sl = cs % RING
                        jj = kt % 4
                        kcol = dc * RING * CW + sl * CW + jj * 128
                        vcols[2 * hh + j] = (sl * 4 + jj) * NH * VST + \
                            (2 * p + hh) * VST
                        mm_tok[hh] = b.emit(
                            "pe", lambda t, kcol=kcol, pb=pb, dc=dc, hh=hh,
                            j=j: t.matmul(
                                SPv[hh][:, j * 512:j * 512 + 512],
                                kr[pb:pb + 64, kcol:kcol + 128],
                                q_sb[pb:pb + 64, dc * CW:(dc + 1) * CW],
                                start=True, stop=True),
                            waits=([uw[hh]] +
                                   ([t_q, t_v] if hh == 0 else []))
                            if j == 0 else [],
                            inc="pe" if j == 1 else None)
                if w == 0 and pending_end[0] is not None:
                    pending_end[0]()
                    pending_end[0] = None
                if pend is not None:
                    emit_avs(pend)
                am_toks = [None] * 4
                for hh in range(2):
                    h = 2 * p + hh
                    t_ex = b.emit(
                        "act", lambda s, hh=hh, half=half, h=h: s.activation(
                            prW[hh][:, half * 1024:half * 1024 + 1024],
                            SPv[hh][:, :], AF.Exp,
                            bias=negM[:, h:h + 1], scale=0.125),
                        waits=[mm_tok[hh], t_negM_evac, t_prw_war[hh]],
                        inc="act")
                    uw[hh] = t_ex
                    t_last_exp = t_ex
                    for j in range(2):
                        kt = kts[2 * w + j]
                        a0 = 384 - (kt * 128 - c * CW)
                        prq = prW[hh][:, half * 1024 + j * 512:
                                      half * 1024 + j * 512 + 512]
                        t_am = b.emit(
                            "dve", lambda v, prq=prq, hh=hh, a0=a0:
                            v.tensor_mul(prq, prq,
                                         a_buf[hh][:, a0:a0 + 512]),
                            waits=[t_ex, t_aload[hh]], inc="dve")
                        am_toks[2 * hh + j] = t_am
                        t_amlast[hh] = t_am
                pend = (w, am_toks, vcols)
            emit_avs(pend)
            for hh in range(2):
                nxt = 2 * p + 2 + hh
                if nxt < NH:
                    t_aload[hh] = b.emit(
                        "sync", lambda s, hh=hh, nxt=nxt: s.dma_start(
                            a_buf[hh][:], A_dram[nxt]),
                        waits=[t_amlast[hh]], inc="hw")
                elif c + 1 < NCH:
                    t_aload[hh] = b.emit(
                        "sync", lambda s, hh=hh: s.dma_start(
                            a_buf[hh][:], A_dram[hh]),
                        waits=[t_amlast[hh]], inc="hw")
            def pair_end(dc=dc, avl=tuple(t_avlast)):
                global_recip = None
                for hh in range(2):
                    pb = 64 * hh
                    b.emit("act", lambda s, hh=hh, avl=avl: s.activation(
                        rr_sb, PP45[hh][64:65, :], AF.Ln),
                        waits=[avl[hh], t_rcm[0]])
                    t_rc = b.emit("act", lambda s: s.activation(
                        rr_bf, rr_sb, AF.Exp, scale=-1.0), inc="act")
                    t_rm = b.emit("pe", lambda t: t.matmul(
                        P7[0:64, :], ones_row_bf[0:1, 0:64], rr_bf,
                        start=True, stop=True),
                        waits=[t_rc, t_rcm[0]], inc="pe")
                    t_ev = b.emit("dve", lambda v, hh=hh, avl=avl:
                                  v.tensor_copy(avraw, PP45[hh][0:64, :]),
                                  waits=[avl[hh], t_rcm[0]], inc="dve")
                    ps45[hh] = t_ev
                    t_rcm[0] = b.emit(
                        "dve", lambda v, dc=dc, pb=pb: v.tensor_tensor(
                            attn_sb[pb:pb + 64, dc * CW:(dc + 1) * CW],
                            avraw, P7[0:64, :], ALU.mult),
                        waits=[t_rm, t_ev, t_attn_read], inc="dve")
            pending_end[0] = pair_end
        pending_end[0]()
        pending_end[0] = None
        t_recip_mult = t_rcm[0]

        t_o = None
        t_sqo = None
        t_sto = None
        for dco in range(ND):
            pi = dco % 2
            ps = PP45[pi]
            for dci in range(ND):
                b.emit("pe", lambda t, dco=dco, dci=dci, ps=ps: t.matmul(
                    ps[:, :],
                    woT[:, dci * D + dco * 128:dci * D + (dco + 1) * 128],
                    attn_sb[:, dci * CW:(dci + 1) * CW],
                    start=(dci == 0), stop=False),
                    waits=[t_recip_mult, t_woT, ps45[pi]] if dci == 0 else [])
            tm = b.emit("pe", lambda t, dco=dco, ps=ps: t.matmul(
                ps[:, :], bo_sb[0:1, dco * 128:(dco + 1) * 128],
                ones_row_bf[0:1, 0:512], start=False, stop=True), inc="pe")
            t_o = b.emit("dve", lambda v, dco=dco, ps=ps: v.tensor_copy(
                h_sb[:, dco * CW:(dco + 1) * CW], ps[:, :]),
                waits=[tm], inc="dve")
            ps45[pi] = t_o
            t_sqo = b.emit("act", lambda s, dco=dco: s.activation(
                sq_sb[:, dco * CW:(dco + 1) * CW],
                h_sb[:, dco * CW:(dco + 1) * CW], AF.Square),
                waits=[t_o] + ([t_recip_mult] if dco == 0 else []),
                inc="act")
            b.emit("pe", lambda t, dco=dco: t.matmul(
                P6[0:1, :], ones_col_bf[:],
                h_sb[:, dco * CW:(dco + 1) * CW],
                start=(dco == 0), stop=(dco == ND - 1)),
                waits=[t_o])
            t_sto = b.emit("pe", lambda t, dco=dco: t.matmul(
                P7[0:1, :], ones_col_bf[:],
                sq_sb[:, dco * CW:(dco + 1) * CW],
                start=(dco == 0), stop=(dco == ND - 1)),
                waits=[t_sqo], inc="pe" if dco == ND - 1 else None)

        t_osq_prev = t_sqo
        t_prev_statq = t_sto
        t_rowso = ln_rows(P6, P7, t_sto, RO_O)
        t_Ro, t_A2o = ln_bcast(RO_O, t_rowso, P6, P7)
        t_p3 = None
        for dc in range(ND):
            b.emit("dve", lambda v, dc=dc: v.tensor_mul(
                tmp0, h_sb[:, dc * CW:(dc + 1) * CW], P6[:, :]),
                waits=[t_Ro, t_A2o,
                       t_y1_store[c - 1] if c > 0 else None]
                if dc == 0 else [])
            b.emit("dve", lambda v: v.tensor_add(tmp1, tmp0, P7[:, :]))
            t_p3 = b.emit("dve", lambda v, dc=dc, xb=xb: v.tensor_add(
                y1c[:, dc * CW:(dc + 1) * CW], tmp1,
                xb[:, dc * CW:(dc + 1) * CW]),
                inc="dve" if dc == ND - 1 else None)
        t_y1_pass3[c % 2] = t_p3
        t_y1_store[c] = b.emit("sync", lambda s, c=c: s.dma_start(
            y1_dram.rearrange("(dc p) t -> p dc t", p=128)
            [:, :, c * CW:(c + 1) * CW], y1c[:]),
            waits=[t_p3], inc="hw")
        t_attn_read = t_y1_store[c]

    b.emit("pe", lambda t: t.matmul(
        P6[0:1, 0:8], ones_row_f32[0:1, 0:1], ones_row_f32[0:1, 0:8],
        start=True, stop=True),
        waits=[t_y1_pass3[(NCH - 1) % 2]], inc="ph")
    b.emit("act", lambda s: s.copy(dummy[0:1, 0:2], ones_row_f32[0:1, 0:2]),
           inc="ph")
    b.emit("dve", lambda v: v.memset(dummy[0:1, 2:4], 0.0), inc="ph")
    b.emit("pool", lambda g: g.memset(dummy[0:1, 4:6], 0.0), inc="ph")
    att.close()
    BARRIER = ("ph", 4)

    ffn = contextlib.ExitStack()
    y1_sb = [ffn.enter_context(nc.sbuf_tensor(f"y1_{i}", [128, ND * CW], BF16))
             for i in range(2)]
    h2_sb = ffn.enter_context(nc.sbuf_tensor("h2_sb", [128, ND * CW], BF16))
    h1_sb = ffn.enter_context(nc.sbuf_tensor("h1_sb", [128, NHC * CW], BF16))
    h1w_sb = ffn.enter_context(nc.sbuf_tensor("h1w", [128, NHC * CW], BF16))
    mix_sb = ffn.enter_context(nc.sbuf_tensor("mix", [128, ND * CW], BF16))
    of_sb = [ffn.enter_context(nc.sbuf_tensor(f"of{i}", [128, 512], F32))
             for i in range(2)]
    Wall = ffn.enter_context(nc.sbuf_tensor("Wall", [128, NE * CW], BF16))
    wrows = ffn.enter_context(nc.sbuf_tensor("wrows", [1, NE * CW], BF16))
    prob_sb = ffn.enter_context(nc.sbuf_tensor("prob", [128, 4 * NE], F32))
    top8 = ffn.enter_context(nc.sbuf_tensor("top8", [128, NE], F32))
    wtm = ffn.enter_context(nc.sbuf_tensor("wtm", [128, 4 * NE], F32))
    rcol = ffn.enter_context(nc.sbuf_tensor("rcol", [128, 8], F32))
    row2 = ffn.enter_context(nc.sbuf_tensor("row2", [1, 1024], F32))
    w1b = [ffn.enter_context(nc.sbuf_tensor(f"w1b{i}", [128, ND * 1024], BF16))
           for i in range(2)]
    w2b = [ffn.enter_context(nc.sbuf_tensor(f"w2b{i}", [128, 8 * D], BF16))
           for i in range(2)]
    rb_sb = ffn.enter_context(nc.sbuf_tensor("rb_sb", [1, NE], BF16))
    sb2_sb = ffn.enter_context(nc.sbuf_tensor("sb2_sb", [1, D], BF16))
    rb2_sb = ffn.enter_context(nc.sbuf_tensor("rb2_sb", [1, NE * D], BF16))
    rw_sb = ffn.enter_context(nc.sbuf_tensor("rw_sb", [128, ND * NE], BF16))
    for nm, srcT, dst in (("rb", rbt, rb_sb), ("sb2", sb2, sb2_sb),
                          ("rb2", rb2, rb2_sb)):
        t_bias[nm] = b.emit("pool", lambda g, a=srcT, d=dst: g.dma_start(
            d[:], a), waits=[BARRIER], inc="ld")
    t_rw = b.emit("pool", lambda g: g.dma_start(
        rw_sb[:], rwt.rearrange("(dc p) e -> p dc e", p=128)),
        waits=[BARRIER], inc="ld")

    def load_w1(eidx, half, waits):
        src = sw1_bf if eidx < 0 else rw1_bf[eidx]
        tok = tcast[4] if eidx < 0 else t_rw1c[eidx]
        return b.emit("sync", lambda s, src=src, half=half: s.dma_start(
            w1b[half][:],
            src.rearrange("(dc p) h1 -> p dc h1", p=128)
            [:, :, half * 1024:(half + 1) * 1024]),
            waits=list(waits) + [tok, BARRIER], inc="hw")

    def load_w2(eidx, half, waits):
        src = sw2_bf if eidx < 0 else rw2_bf[eidx]
        tok = tcast[5] if eidx < 0 else t_rw2c[eidx]
        return b.emit("sync", lambda s, src=src, half=half: s.dma_start(
            w2b[half][:],
            src.rearrange("(hc p) f -> p hc f", p=128)
            [:, half * 8:(half + 1) * 8, :]),
            waits=list(waits) + [tok, BARRIER], inc="hw")

    t_pool_tok = {}
    t_h1_done = {}
    t_h1_read = {}
    t_mixadd = {}
    t_final = [None, None]
    t_of_war = {0: None, 1: None}
    t_h1ps_war = {0: None, 1: None}
    t_p0_war = []
    t_p1_war = None
    t_wtm_war = None
    t_mix_sq = None
    t_wall_pool_read = None
    t_w1_load = {}
    t_w2_load = {}
    t_sq_war = t_prev_statq
    t_final_prev = None
    t_h2 = None

    t_w1_load[(0, -1)] = (load_w1(-1, 0, []), load_w1(-1, 1, []))
    t_w2_load[(0, -1)] = (load_w2(-1, 0, []), load_w2(-1, 1, []))

    t_y1ld = {}
    t_y1ld[0] = b.emit("sync", lambda s: s.dma_start(
        y1_sb[0][:], y1_dram.rearrange("(dc p) t -> p dc t", p=128)
        [:, :, 0:CW]), waits=[t_y1_store[0], BARRIER], inc="hw")
    hst = {"sq_war": t_sq_war, "h2": None, "R2": {}, "A22": {},
           "sumsq": {}}

    def h2_stats(tcx):
        yb = y1_sb[tcx % 2]
        t_sq = b.emit("act", lambda s, yb=yb: s.activation(
            sq_sb[:], yb[:], AF.Square),
            waits=[t_y1ld[tcx], hst["sq_war"], hst["h2"]], inc="act")
        tok = None
        for dc in range(ND):
            b.emit("pe", lambda t, dc=dc, yb=yb: t.matmul(
                P6[0:1, :], ones_col_bf[:], yb[:, dc * CW:(dc + 1) * CW],
                start=(dc == 0), stop=(dc == ND - 1)),
                waits=[t_y1ld[tcx], hst["h2"]] if dc == 0 else [])
        for dc in range(ND):
            tok = b.emit("pe", lambda t, dc=dc: t.matmul(
                P7[0:1, :], ones_col_bf[:], sq_sb[:, dc * CW:(dc + 1) * CW],
                start=(dc == 0), stop=(dc == ND - 1)),
                waits=[t_sq] if dc == 0 else [],
                inc="pe" if dc == ND - 1 else None)
        hst["sumsq"][tcx] = tok
        t_rows2 = ln_rows(P6, P7, tok, RO_X)
        t_R2, t_A22 = ln_bcast(RO_X, t_rows2, P6, P7)
        hst["R2"][tcx] = t_R2
        hst["A22"][tcx] = t_A22

    h2_stats(0)
    for tc in range(NCH):
        yb = y1_sb[tc % 2]
        t_y1 = t_y1ld[tc]
        t_R2 = hst["R2"][tc]
        t_A22 = hst["A22"][tc]
        t_h2 = None
        war_h2 = t_h1_done.get((tc - 1, NE - 1))
        for dc in range(ND):
            b.emit("dve", lambda v, dc=dc, yb=yb: v.tensor_mul(
                tmp0, yb[:, dc * CW:(dc + 1) * CW], P6[:, :]),
                waits=[t_R2, t_A22, war_h2] if dc == 0 else [])
            t_h2 = b.emit("dve", lambda v, dc=dc: v.tensor_add(
                h2_sb[:, dc * CW:(dc + 1) * CW], tmp0, P7[:, :]),
                inc="dve" if dc == ND - 1 else None)
        hst["h2"] = t_h2

        for ei in range(-1, NE):
            lw1 = t_w1_load[(tc, ei)]
            t_last = None
            for hc in range(NHC):
                half = hc // 8
                ps = PSA[:, (hc % 2) * 512:(hc % 2) * 512 + 512]
                for dci in range(ND):
                    col = half * 0 + (hc % 8) * 128 + dci * 1024
                    b.emit("pe", lambda t, col=col, dci=dci, half=half, ps=ps:
                           t.matmul(ps, w1b[half][:, col:col + 128],
                                    h2_sb[:, dci * CW:(dci + 1) * CW],
                                    start=(dci == 0), stop=(dci == ND - 1)),
                           waits=([lw1[half], t_h2, t_h1ps_war[hc % 2],
                                   t_sb1c, t_rb1c] +
                                  (t_p0_war if hc < 2 else []))
                           if dci == 0 else [],
                           inc="pe" if dci == ND - 1 else None)
                tok_mm = ("pe", b.sems["pe"][1])
                bc = (sb1c[:, hc:hc + 1] if ei < 0
                      else rb1c[:, ei * NHC + hc:ei * NHC + hc + 1])
                dst = h1w_sb if ei < 0 else h1_sb
                if ei < 0:
                    war_dst = t_h1_read.get((tc - 1, NE - 1))
                elif ei == 0:
                    war_dst = t_pool_tok.get((tc - 1, NE - 1, hc))
                else:
                    war_dst = t_pool_tok.get((tc, ei - 1, hc))
                t_g = b.emit("act", lambda s, hc=hc, ps=ps, bc=bc, dst=dst:
                             s.activation(dst[:, hc * CW:(hc + 1) * CW],
                                          ps, AF.Gelu, bias=bc),
                             waits=[tok_mm, war_dst], inc="act")
                t_h1ps_war[hc % 2] = t_g
                if ei >= 0:
                    t_p = b.emit("pool", lambda g, hc=hc, ei=ei: g.tensor_mul(
                        h1w_sb[:, hc * CW:(hc + 1) * CW],
                        h1_sb[:, hc * CW:(hc + 1) * CW],
                        Wall[:, ei * CW:(ei + 1) * CW]),
                        waits=[t_g, t_wall, t_h1_read.get((tc, ei - 1))],
                        inc="pool")
                else:
                    t_p = t_g
                t_pool_tok[(tc, ei, hc)] = t_p
                t_last = t_p
            t_h1_done[(tc, ei)] = t_last
            if ei == -1:
                t_rmm = None
                for tt in range(4):
                    for dci in range(ND):
                        b.emit("pe", lambda t, tt=tt, dci=dci,
                               s=(tt == 0 and dci == 0): t.matmul(
                                   PSA[:, tt * 8:(tt + 1) * 8],
                                   h2_sb[:, dci * CW + tt * 128:dci * CW + (tt + 1) * 128],
                                   rw_sb[:, dci * 8:(dci + 1) * 8],
                                   start=s, stop=False),
                               waits=[t_h2, t_rw, t_bias['rb']] + t_p0_war
                               if (tt == 0 and dci == 0) else [])
                    t_rmm = b.emit("pe", lambda t, tt=tt: t.matmul(
                        PSA[:, tt * 8:(tt + 1) * 8],
                        ones_row_bf[0:1, tt * 128:(tt + 1) * 128],
                        rb_sb[:], start=False, stop=(tt == 3)),
                        inc="pe" if tt == 3 else None)
                t_w_tm = None
                t_e = None
                t_red = None
                for tt in range(4):
                    sl = slice(tt * 8, (tt + 1) * 8)
                    t_red = b.emit("dve", lambda v, sl=sl: v.reduce_max(
                        top8[:, 0:1], PSA[:, sl], axis=AX.X, negate=True),
                        waits=[t_rmm, t_e], inc="dve")
                    t_e = b.emit("act", lambda s, sl=sl: s.activation(
                        prob_sb[:, sl], PSA[:, sl], AF.Exp, bias=top8[:, 0:1]),
                        waits=[t_red], inc="act")
                    b.emit("dve", lambda v, sl=sl, tt=tt: v.reduce_sum(
                        rcol[:, tt:tt + 1], prob_sb[:, sl], axis=AX.X), waits=[t_e])
                    t_mx = b.emit("dve", lambda v, sl=sl: v.max(
                        out=top8[:], in_=prob_sb[:, sl]), inc="dve")
                    t_cp = b.emit("dve", lambda v, tt=tt: v.tensor_copy(
                        rcol[:, 4 + tt:5 + tt], top8[:, 1:2]),
                        waits=[t_mx], force=True, inc="dve")
                    t_ge = b.emit("dve", lambda v, sl=sl, tt=tt: v.tensor_single_scalar(
                        wtm[:, sl], prob_sb[:, sl], rcol[:, 4 + tt:5 + tt], ALU.is_ge),
                        waits=[t_cp, t_wtm_war], force=True, inc="dve")
                    t_w_tm = b.emit("dve", lambda v, sl=sl: v.tensor_mul(
                        wtm[:, sl], wtm[:, sl], prob_sb[:, sl]),
                        waits=[t_ge], force=True, inc="dve")
                t_p0_war = [t_e, t_red]
                if tc + 1 < NCH:
                    t_y1ld[tc + 1] = b.emit(
                        "sync", lambda s, tc=tc: s.dma_start(
                            y1_sb[(tc + 1) % 2][:],
                            y1_dram.rearrange("(dc p) t -> p dc t", p=128)
                            [:, :, (tc + 1) * CW:(tc + 2) * CW]),
                        waits=[t_y1_store[tc + 1], t_final[(tc + 1) % 2]],
                        inc="hw")
            t_wall_pool_read = t_last if ei >= 0 else t_wall_pool_read
            if ei + 1 < NE:
                t_w1_load[(tc, ei + 1)] = (
                    load_w1(ei + 1, 0, [t_h1_done[(tc, ei)]]),
                    load_w1(ei + 1, 1, []))
            elif tc + 1 < NCH:
                t_w1_load[(tc + 1, -1)] = (
                    load_w1(-1, 0, [t_h1_done[(tc, ei)]]),
                    load_w1(-1, 1, []))
            lw2 = t_w2_load[(tc, ei)]
            for fc in range(ND):
                pi = fc % 2
                ps = PP45[pi]
                for hc in range(NHC):
                    half = hc // 8
                    col = (hc % 8) * D + fc * 128
                    b.emit("pe", lambda t, hc=hc, col=col, half=half, ps=ps:
                           t.matmul(ps[:, :], w2b[half][:, col:col + 128],
                                    h1w_sb[:, hc * CW:(hc + 1) * CW],
                                    start=(hc == 0), stop=False),
                           waits=[lw2[half], t_pool_tok[(tc, ei, hc)]] +
                           ([ps45[pi]] if hc == 0 else []))
                if ei < 0:
                    tmb = b.emit("pe", lambda t, fc=fc, ps=ps: t.matmul(
                        ps[:, :], sb2_sb[0:1, fc * 128:(fc + 1) * 128],
                        ones_row_bf[0:1, 0:512], start=False, stop=True),
                        waits=[t_bias['sb2']], inc="pe")
                else:
                    tmb = b.emit("pe", lambda t, fc=fc, ps=ps, ei=ei: t.matmul(
                        ps[:, :],
                        rb2_sb[0:1, ei * D + fc * 128:ei * D + (fc + 1) * 128],
                        wrows[0:1, ei * CW:(ei + 1) * CW],
                        start=False, stop=True),
                        waits=[t_bias['rb2'], t_wall], inc="pe")
                if ei < 0:
                    t_ma = b.emit("dve", lambda v, fc=fc, ps=ps: v.tensor_copy(
                        mix_sb[:, fc * CW:(fc + 1) * CW], ps[:, :]),
                        waits=[tmb, t_mix_sq], inc="dve")
                else:
                    t_ma = b.emit("dve", lambda v, fc=fc, ps=ps: v.tensor_add(
                        mix_sb[:, fc * CW:(fc + 1) * CW],
                        mix_sb[:, fc * CW:(fc + 1) * CW], ps[:, :]),
                        waits=[tmb], inc="dve")
                t_mixadd[(tc, ei, fc)] = t_ma
                ps45[pi] = t_ma
                if ei == NE - 1:
                    t_sqm = b.emit("act", lambda s, fc=fc: s.activation(
                        sq_sb[:, fc * CW:(fc + 1) * CW],
                        mix_sb[:, fc * CW:(fc + 1) * CW], AF.Square),
                        waits=[t_ma] +
                        ([t_R2, hst["sumsq"].get(tc + 1)] if fc == 0 else []),
                        inc="act")
                    b.emit("pe", lambda t, fc=fc: t.matmul(
                        PSB[0:1, 512:1024], ones_col_bf[:],
                        mix_sb[:, fc * CW:(fc + 1) * CW],
                        start=(fc == 0), stop=(fc == ND - 1)),
                        waits=[t_ma] + ([t_final_prev] if fc == 0 else []))
                    t_stm = b.emit("pe", lambda t, fc=fc: t.matmul(
                        PSB[0:1, 0:512], ones_col_bf[:],
                        sq_sb[:, fc * CW:(fc + 1) * CW],
                        start=(fc == 0), stop=(fc == ND - 1)),
                        waits=[t_sqm] + ([t_wall] if fc == 0 else []),
                        inc="pe" if fc == ND - 1 else None)
            t_h1_read[(tc, ei)] = t_mixadd[(tc, ei, ND - 1)]
            if ei == -1:
                ttok = None
                for tt in range(4):
                    ttok = b.emit("pe", lambda t, tt=tt: t.transpose(
                        PSB[0:1, tt * 128:(tt + 1) * 128],
                        rcol[:, tt:tt + 1], eye_f32[:]),
                        waits=[t_w_tm, t_eye, t_p1_war] if tt == 0 else [],
                        inc="pe" if tt == 3 else None)
                t_sr = b.emit("dve", lambda v: v.tensor_copy(
                    row2[0:1, 0:512], PSB[0:1, 0:512]), waits=[ttok], inc="dve")
                b.emit("act", lambda s: s.activation(
                    row2[0:1, 512:1024], row2[0:1, 0:512], AF.Ln), waits=[t_sr])
                t_rcp = b.emit("act", lambda s: s.activation(
                    row2[0:1, 512:1024], row2[0:1, 512:1024], AF.Exp, scale=-1.0),
                    inc="act")
                t_prev = t_sr
                for e in range(NE):
                    tt_tok = None
                    for tt in range(4):
                        tt_tok = b.emit("pe", lambda t, e=e, tt=tt: t.transpose(
                            PSB[0:1, tt * 128:(tt + 1) * 128],
                            wtm[:, tt * 8 + e:tt * 8 + e + 1],
                            eye_f32[:]),
                            waits=[t_w_tm, t_prev] if tt == 0 else [],
                            inc="pe" if tt == 3 else None)
                    t_prev = b.emit("dve", lambda v, e=e: v.tensor_copy(
                        wrows[0:1, e * CW:(e + 1) * CW], PSB[0:1, 0:512]),
                        waits=[tt_tok], inc="dve")
                t_wtm_war = tt_tok
                t_wn = None
                for e in range(NE):
                    t_wn = b.emit("dve", lambda v, e=e: v.tensor_mul(
                        wrows[0:1, e * CW:(e + 1) * CW],
                        wrows[0:1, e * CW:(e + 1) * CW], row2[0:1, 512:1024]),
                        waits=[t_rcp], force=True,
                        inc="dve" if e == NE - 1 else None)
                t_prev2 = t_prev
                t_wall = None
                for e in range(NE):
                    tb = b.emit("pe", lambda t, e=e: t.matmul(
                        PSB[0:64, 0:512], ones_row_bf[0:1, 0:64],
                        wrows[0:1, e * CW:(e + 1) * CW], start=True, stop=True),
                        waits=[t_wn, t_prev2], inc="pe")
                    tb2 = b.emit("pe", lambda t, e=e: t.matmul(
                        PSB[64:128, 0:512], ones_row_bf[0:1, 0:64],
                        wrows[0:1, e * CW:(e + 1) * CW], start=True, stop=True),
                        inc="pe")
                    t_wall = b.emit("dve", lambda v, e=e: v.tensor_copy(
                        Wall[:, e * CW:(e + 1) * CW], PSB[:, 0:512]),
                        waits=[tb2, t_wall_pool_read], inc="dve")
                    t_prev2 = t_wall
                t_p1_war = t_wall
                if tc + 1 < NCH:
                    h2_stats(tc + 1)

            if ei + 1 < NE:
                t_w2_load[(tc, ei + 1)] = (
                    load_w2(ei + 1, 0, [t_h1_read[(tc, ei)]]),
                    load_w2(ei + 1, 1, []))
            elif tc + 1 < NCH:
                t_w2_load[(tc + 1, -1)] = (
                    load_w2(-1, 0, [t_h1_read[(tc, ei)]]),
                    load_w2(-1, 1, []))

        t_mix_sq = t_sqm
        hst["sq_war"] = t_stm
        t_rows3 = ln_rows(PSB[:, 512:1024], PSB[:, 0:512], t_stm, RO_O)
        t_R3, t_A23 = ln_bcast(RO_O, t_rows3, PSB[:, 512:1024],
                               PSB[:, 0:512])
        for dc in range(ND):
            ob = of_sb[dc % 2]
            b.emit("dve", lambda v, dc=dc: v.tensor_mul(
                tmp0, mix_sb[:, dc * CW:(dc + 1) * CW], PSB[:, 512:1024]),
                waits=[t_R3, t_A23, t_of_war[dc % 2]]
                if dc < 2 else [t_of_war[dc % 2]])
            b.emit("dve", lambda v: v.tensor_add(tmp1, tmp0, PSB[:, 0:512]))
            t_f = b.emit("dve", lambda v, dc=dc, ob=ob, yb=yb: v.tensor_add(
                ob[:], tmp1, yb[:, dc * CW:(dc + 1) * CW]), inc="dve")
            if dc == ND - 1:
                t_final[tc % 2] = t_f
                t_final_prev = t_f
                t_p1_war = t_f
            t_of_war[dc % 2] = b.emit(
                "sync", lambda s, tc=tc, dc=dc, ob=ob: s.dma_start(
                    yt.rearrange("(dc p) t -> p dc t", p=128)
                    [:, dc, tc * CW:(tc + 1) * CW], ob[:]),
                waits=[t_f], inc="hw")

    b.emit("sync", lambda s: s.wait_ge(b.sems["hw"][0], b.sems["hw"][1]))
    ffn.close()

    b.realize()
    b.es.close()
    import sys
    print(f"[kernel build] ops: " +
          ", ".join(f"{e}={len(b.ops[e])}" for e in ENGS), file=sys.stderr)
    return nc


_NC_CACHE = {}


def kernel(**inputs):
    x = np.asarray(inputs["L0_attn_x"], dtype=np.float32)

    def f(name):
        return np.ascontiguousarray(np.asarray(inputs[name], dtype=np.float32))

    common = {
        "wq": f("L0_attn_wq"), "wk": f("L0_attn_wk"), "wv": f("L0_attn_wv"),
        "wo": f("L0_attn_wo"),
        "bq": f("L0_attn_bq").reshape(1, D), "bk": f("L0_attn_bk").reshape(1, D),
        "bv": f("L0_attn_bv").reshape(1, D), "bo": f("L0_attn_bo").reshape(1, D),
        "rwt": f("L0_ffn_router_w"), "rbt": f("L0_ffn_router_b").reshape(1, NE),
        "sw1": f("L0_ffn_sw1"), "sb1": f("L0_ffn_sb1").reshape(1, HID),
        "sw2": f("L0_ffn_sw2"), "sb2": f("L0_ffn_sb2").reshape(1, D),
        "rw1": f("L0_ffn_rw1"), "rb1": f("L0_ffn_rb1").reshape(1, NE * HID),
        "rw2": f("L0_ffn_rw2"), "rb2": f("L0_ffn_rb2").reshape(1, NE * D),
    }
    if "nc" not in _NC_CACHE:
        _NC_CACHE["nc"] = build()
    nc = _NC_CACHE["nc"]
    in_maps = []
    for bi in range(8):
        m = dict(common)
        m["xt"] = np.ascontiguousarray(x[bi].T)
        in_maps.append(m)
    res = run_bass_kernel_spmd(nc, in_maps, core_ids=list(range(8)))
    out = np.stack([np.asarray(r["yt"]).T for r in res.results], axis=0)
    return np.ascontiguousarray(out.astype(np.float32))


# version 19
# speedup vs baseline: 1.0610x; 1.0610x over previous
import contextlib
import math
import numpy as np
import concourse.bass as bass
import concourse.mybir as mybir
from concourse.bass_utils import run_bass_kernel_spmd

F32 = mybir.dt.float32
BF16 = mybir.dt.bfloat16
FP16 = mybir.dt.float16
AF = mybir.ActivationFunctionType
ALU = mybir.AluOpType
AX = mybir.AxisListType

T = 4096
D = 1024
HID = 2048
NE = 8
NH = 16
DH = 64
WIN = 1024
CW = 512
NCH = T // CW
ND = D // 128
NHC = HID // 128
RING = 3
VST = 65
EPS = 1e-5
ALIBI_START = 2.0 ** (-2.0 ** -(math.log2(NH) - 3.0))
SLOPES = [ALIBI_START * ALIBI_START ** i for i in range(NH)]

ENGS = ("pe", "act", "dve", "pool", "sync")


class B:
    def __init__(self, nc):
        self.nc = nc
        self.ops = {e: [] for e in ENGS}
        self.es = contextlib.ExitStack()
        self.sems = {}
        for name, owner in (("pe", "pe"), ("act", "act"), ("dve", "dve"),
                            ("pool", "pool"), ("ld", None), ("hw", None),
                            ("ph", None)):
            h = self.es.enter_context(nc.semaphore(name))
            self.sems[name] = [h, 0, owner]

    def emit(self, eng, fn, waits=(), inc=None, force=False):
        ws = []
        for w in waits:
            if w is None:
                continue
            sem, val = w
            if self.sems[sem][2] == eng and not force:
                continue
            ws.append((sem, val))
        tok = None
        if inc is not None:
            s = self.sems[inc]
            s[1] += (16 if inc in ("ld", "hw") else 1)
            tok = (inc, s[1])
        self.ops[eng].append((ws, fn, inc))
        return tok

    def realize(self):
        hiw = {e: {} for e in ENGS}

        def run_engine(eng, obj):
            hw = hiw[eng]
            for ws, fn, inc in self.ops[eng]:
                for sem, val in ws:
                    if hw.get(sem, 0) >= val:
                        continue
                    hw[sem] = val
                    obj.wait_ge(self.sems[sem][0], val)
                ins = fn(obj)
                if inc is not None:
                    ins.then_inc(self.sems[inc][0],
                                 16 if inc in ("ld", "hw") else 1)

        with self.nc.Block() as blk:
            blk.tensor(lambda t: run_engine("pe", t))
            blk.scalar(lambda s: run_engine("act", s))
            blk.vector(lambda v: run_engine("dve", v))
            blk.gpsimd(lambda g: run_engine("pool", g))
            blk.sync(lambda s: run_engine("sync", s))


def build():
    nc = bass.Bass("TRN2", target_bir_lowering=False)
    xt = nc.dram_tensor("xt", [D, T], F32, kind="ExternalInput").ap()
    wq = nc.dram_tensor("wq", [D, D], F32, kind="ExternalInput").ap()
    wk = nc.dram_tensor("wk", [D, D], F32, kind="ExternalInput").ap()
    wv = nc.dram_tensor("wv", [D, D], F32, kind="ExternalInput").ap()
    wo = nc.dram_tensor("wo", [D, D], F32, kind="ExternalInput").ap()
    bq = nc.dram_tensor("bq", [1, D], F32, kind="ExternalInput").ap()
    bk = nc.dram_tensor("bk", [1, D], F32, kind="ExternalInput").ap()
    bv = nc.dram_tensor("bv", [1, D], F32, kind="ExternalInput").ap()
    bo = nc.dram_tensor("bo", [1, D], F32, kind="ExternalInput").ap()
    rwt = nc.dram_tensor("rwt", [D, NE], F32, kind="ExternalInput").ap()
    rbt = nc.dram_tensor("rbt", [1, NE], F32, kind="ExternalInput").ap()
    sw1 = nc.dram_tensor("sw1", [D, HID], F32, kind="ExternalInput").ap()
    sb1 = nc.dram_tensor("sb1", [1, HID], F32, kind="ExternalInput").ap()
    sw2 = nc.dram_tensor("sw2", [HID, D], F32, kind="ExternalInput").ap()
    sb2 = nc.dram_tensor("sb2", [1, D], F32, kind="ExternalInput").ap()
    rw1 = nc.dram_tensor("rw1", [NE, D, HID], F32, kind="ExternalInput").ap()
    rb1 = nc.dram_tensor("rb1", [1, NE * HID], F32, kind="ExternalInput").ap()
    rw2 = nc.dram_tensor("rw2", [NE, HID, D], F32, kind="ExternalInput").ap()
    rb2 = nc.dram_tensor("rb2", [1, NE * D], F32, kind="ExternalInput").ap()
    yt = nc.dram_tensor("yt", [D, T], F32, kind="ExternalOutput").ap()

    wq_bf = nc.dram_tensor("wq_bf", [D, D], BF16, kind="Internal").ap()
    wk_bf = nc.dram_tensor("wk_bf", [D, D], BF16, kind="Internal").ap()
    wv_bf = nc.dram_tensor("wv_bf", [D, D], BF16, kind="Internal").ap()
    wo_bf = nc.dram_tensor("wo_bf", [D, D], BF16, kind="Internal").ap()
    sw1_bf = nc.dram_tensor("sw1_bf", [D, HID], BF16, kind="Internal").ap()
    sw2_bf = nc.dram_tensor("sw2_bf", [HID, D], BF16, kind="Internal").ap()
    rw1_bf = nc.dram_tensor("rw1_bf", [NE, D, HID], BF16, kind="Internal").ap()
    rw2_bf = nc.dram_tensor("rw2_bf", [NE, HID, D], BF16, kind="Internal").ap()
    y1_dram = nc.dram_tensor("y1_dram", [D, T], BF16, kind="Internal").ap()
    A_dram = nc.dram_tensor("A_dram", [NH, 128, 1920], BF16, kind="Internal").ap()

    b = B(nc)
    es = b.es

    PSA = es.enter_context(nc.psum_tensor("PSA", [128, 1024], F32))
    PSB = es.enter_context(nc.psum_tensor("PSB", [128, 1024], F32))
    PP45 = [es.enter_context(nc.psum_tensor(f"P{i}", [128, 512], F32))
            for i in (4, 5)]
    P6 = es.enter_context(nc.psum_tensor("P6", [128, 512], F32))
    P7 = es.enter_context(nc.psum_tensor("P7", [128, 512], F32))
    sq_sb = es.enter_context(nc.sbuf_tensor("sq_sb", [128, ND * CW], BF16))
    tmp0t = es.enter_context(nc.sbuf_tensor("tmp0t", [128, 512], BF16))
    tmp1t = es.enter_context(nc.sbuf_tensor("tmp1t", [128, 512], BF16))
    tmp0 = tmp0t[:, :]
    tmp1 = tmp1t[:, :]
    tmp_alias = True
    rows = es.enter_context(nc.sbuf_tensor("rows", [1, 3200], F32))
    ones_row_bf = es.enter_context(nc.sbuf_tensor("ones_row_bf", [1, 512], BF16))
    ones_row_f32 = es.enter_context(nc.sbuf_tensor("ones_row_f32", [1, 128], F32))
    ones_col_bf = es.enter_context(nc.sbuf_tensor("ones_col_bf", [128, 1], BF16))
    eye_f32 = es.enter_context(nc.sbuf_tensor("eye_f32", [128, 128], F32))
    iot = es.enter_context(nc.sbuf_tensor("iot", [128, 128], F32))
    bq_sb = es.enter_context(nc.sbuf_tensor("bq_sb", [1, D], BF16))
    bk_sb = es.enter_context(nc.sbuf_tensor("bk_sb", [1, D], BF16))
    bv_sb = es.enter_context(nc.sbuf_tensor("bv_sb", [1, D], BF16))
    bo_sb = es.enter_context(nc.sbuf_tensor("bo_sb", [1, D], BF16))
    sb1c = es.enter_context(nc.sbuf_tensor("sb1c", [128, NHC], F32))
    rb1c = es.enter_context(nc.sbuf_tensor("rb1c", [128, NE * NHC], F32))
    dummy = es.enter_context(nc.sbuf_tensor("dmy_bar", [1, 8], F32))

    RO_X, RO_O = 0, 1024
    ROW_M, ROW_MSQ = 2048, 2560
    NQ, NK, KC, MM_ = 3072, 3088, 3136, 3152

    tcast = []
    for src, dst in ((wq, wq_bf), (wk, wk_bf), (wv, wv_bf), (wo, wo_bf),
                     (sw1, sw1_bf), (sw2, sw2_bf)):
        tcast.append(b.emit("pool", lambda g, s=src, d=dst: g.dma_start(d, s),
                            inc="ld"))
    t_rw1c, t_rw2c = [], []
    for e in range(NE):
        t_rw1c.append(b.emit("pool", lambda g, e=e: g.dma_start(rw1_bf[e], rw1[e]),
                             inc="ld"))
        t_rw2c.append(b.emit("pool", lambda g, e=e: g.dma_start(rw2_bf[e], rw2[e]),
                             inc="ld"))
    t_bias = {}
    for nm, srcT, dst in (("bq", bq, bq_sb), ("bk", bk, bk_sb),
                          ("bv", bv, bv_sb), ("bo", bo, bo_sb)):
        t_bias[nm] = b.emit("pool", lambda g, a=srcT, d=dst: g.dma_start(d[:], a),
                            inc="ld")

    b.emit("dve", lambda v: v.memset(ones_row_bf[:], 1.0))
    b.emit("dve", lambda v: v.memset(ones_row_f32[:], 1.0))
    b.emit("dve", lambda v: v.memset(ones_col_bf[:], 1.0))
    t_iot = b.emit("pool", lambda g: g.iota(
        iot[:], pattern=[[-1, 128]], base=0, channel_multiplier=1,
        allow_small_or_imprecise_dtypes=True), inc="pool")
    t_eye = b.emit("dve", lambda v: v.tensor_single_scalar(
        eye_f32[:], iot[:], 0.0, ALU.is_equal), waits=[t_iot], inc="dve")

    def bias_cols(flat_sb, out_sb, n, tok_ld, psum):
        toks = []
        for g0 in range(0, n, 4):
            gn = min(4, n - g0)
            for j in range(gn):
                idx = g0 + j
                toks.append(b.emit(
                    "pe",
                    lambda t, i=idx, j=j, s=(j == 0), e=(j == gn - 1):
                    t.matmul(psum[0:128, j:j + 1],
                             flat_sb[0:1, i * 128:(i + 1) * 128],
                             ones_row_f32[0:1, 0:1], start=s, stop=e),
                    waits=([tok_ld, toks[-1] if toks else None]
                           if j == 0 else []),
                    inc="pe" if j == gn - 1 else None))
            toks.append(b.emit(
                "dve",
                lambda v, g0=g0, gn=gn: v.tensor_copy(
                    out_sb[:, g0:g0 + gn], psum[0:128, 0:gn]),
                waits=[toks[-1]], inc="dve"))
        return toks[-1]

    pre = contextlib.ExitStack()
    sb1_sb = pre.enter_context(nc.sbuf_tensor("sb1_sb", [1, HID], F32))
    rb1_sb = pre.enter_context(nc.sbuf_tensor("rb1_sb", [1, NE * HID], F32))
    t_sb1f = b.emit("sync", lambda s: s.dma_start(sb1_sb[:], sb1), inc="hw")
    t_rb1f = b.emit("sync", lambda s: s.dma_start(rb1_sb[:], rb1), inc="hw")
    t_sb1c = bias_cols(sb1_sb, sb1c, NHC, t_sb1f, P6)
    t_rb1c = bias_cols(rb1_sb, rb1c, NE * NHC, t_rb1f, P7)
    pre.close()

    att = contextlib.ExitStack()
    wqT = att.enter_context(nc.sbuf_tensor("wqT", [128, ND * D], BF16))
    wkT = att.enter_context(nc.sbuf_tensor("wkT", [128, ND * D], BF16))
    wvT = att.enter_context(nc.sbuf_tensor("wvT", [128, ND * D], BF16))
    woT = att.enter_context(nc.sbuf_tensor("woT", [128, ND * D], BF16))
    kr = att.enter_context(nc.sbuf_tensor("kr", [128, ND * RING * CW], BF16))
    vr = att.enter_context(nc.sbuf_tensor("vr", [128, 4 * RING * NH * VST], BF16))
    x_sb = [att.enter_context(nc.sbuf_tensor(f"x{i}", [128, ND * CW], BF16))
            for i in range(2)]
    h_sb = att.enter_context(nc.sbuf_tensor("h_sb", [128, ND * CW], BF16))
    q_sb = att.enter_context(nc.sbuf_tensor("q_sb", [128, ND * CW], BF16))
    attn_sb = att.enter_context(nc.sbuf_tensor("attn_sb", [128, ND * CW], BF16))
    y1c = attn_sb
    prW = [att.enter_context(nc.sbuf_tensor(f"pr{i}", [128, 2048], BF16))
           for i in range(2)]
    a_sb = att.enter_context(nc.sbuf_tensor("a0", [128, 1920], BF16))
    d_sb = att.enter_context(nc.sbuf_tensor("d_sb", [128, 1920], BF16))
    a_buf = [a_sb, d_sb]
    negM = att.enter_context(nc.sbuf_tensor("negM", [128, NH], F32))
    avraw = sq_sb[0:64, 0:512]
    rr_sb = sq_sb[0:1, 1024:2048].bitcast(F32)
    rr_bf = sq_sb[0:1, 2048:2560]

    t_d0 = b.emit("pool", lambda g: g.iota(
        d_sb[:], pattern=[[-1, 1920]], base=384, channel_multiplier=1,
        allow_small_or_imprecise_dtypes=True), inc="pool")
    t_d1 = b.emit("pool", lambda g: g.affine_select(
        d_sb[:], d_sb[:], pattern=[[1, 1920]], base=-384,
        channel_multiplier=-1, compare_op=ALU.is_ge, fill=-60000.0),
        waits=[t_d0], inc="pool")
    t_d = b.emit("pool", lambda g: g.affine_select(
        d_sb[:], d_sb[:], pattern=[[-1, 1920]], base=384 + 1023,
        channel_multiplier=1, compare_op=ALU.is_ge, fill=-60000.0),
        waits=[t_d1], inc="pool")
    t_ast = None
    for h in range(NH):
        t_ae = b.emit("act", lambda s, h=h: s.activation(
            a_sb[:], d_sb[:], AF.Exp, scale=float(SLOPES[h])),
            waits=[t_d, t_ast], inc="act")
        t_ast = b.emit("sync", lambda s, h=h: s.dma_start(A_dram[h], a_sb[:]),
                       waits=[t_ae], inc="hw")
    t_abuild_done = t_ast

    t_wT = []
    for w_src, w_dst, tok in ((wq_bf, wqT, tcast[0]), (wk_bf, wkT, tcast[1]),
                              (wv_bf, wvT, tcast[2]), (wo_bf, woT, tcast[3])):
        for dc in range(ND):
            t_wT.append(b.emit(
                "sync",
                lambda s, src=w_src, dst=w_dst, dc=dc: s.dma_start(
                    dst[:, dc * D:(dc + 1) * D],
                    src[:, dc * 128:(dc + 1) * 128], transpose=True),
                waits=[tok, t_sb1c, t_rb1c], inc="hw"))
    t_wqT, t_wkT, t_wvT, t_woT = t_wT[7], t_wT[15], t_wT[23], t_wT[31]

    t_vones = b.emit("dve", lambda v: v.memset(
        vr.rearrange("p (sg e) -> p sg e", e=VST)[:, :, 64:65], 1.0),
        inc="dve")

    def ln_stats(src_ap, n_free, tok_src, ps_s, ps_q, sq_war, extra=()):
        t_sq = b.emit("act", lambda s: s.activation(
            sq_sb[:], src_ap(None), AF.Square),
            waits=[tok_src, sq_war] + list(extra), inc="act")
        tok = None
        for dc in range(ND):
            b.emit("pe", lambda t, dc=dc: t.matmul(
                ps_s[0:1, :], ones_col_bf[:], src_ap(dc),
                start=(dc == 0), stop=(dc == ND - 1)),
                waits=[tok_src] + list(extra) if dc == 0 else [],
                inc="pe" if dc == ND - 1 else None)
        for dc in range(ND):
            tok = b.emit("pe", lambda t, dc=dc: t.matmul(
                ps_q[0:1, :], ones_col_bf[:],
                sq_sb[:, dc * CW:(dc + 1) * CW],
                start=(dc == 0), stop=(dc == ND - 1)),
                waits=[t_sq] + list(extra) if dc == 0 else [],
                inc="pe" if dc == ND - 1 else None)
        return t_sq, tok

    ln_state = {"act": None}

    def ln_rows(ps_s, ps_q, tok_stats, ro):
        m = rows[0:1, ROW_M:ROW_M + 512]
        msq = rows[0:1, ROW_MSQ:ROW_MSQ + 512]
        rstd = rows[0:1, ro:ro + 512]
        a2 = rows[0:1, ro + 512:ro + 1024]
        b.emit("dve", lambda v: v.tensor_single_scalar(
            m, ps_s[0:1, :], 1.0 / D, ALU.mult),
            waits=[tok_stats, ln_state["act"]])
        b.emit("dve", lambda v: v.tensor_scalar(
            msq, ps_q[0:1, :], 1.0 / D, EPS, ALU.mult, ALU.add))
        b.emit("dve", lambda v: v.tensor_mul(rstd, m, m))
        t_var = b.emit("dve", lambda v: v.tensor_sub(msq, msq, rstd),
                       inc="dve")
        b.emit("act", lambda s: s.activation(rstd, msq, AF.Ln),
               waits=[t_var])
        t_ex = b.emit("act", lambda s: s.activation(
            rstd, rstd, AF.Exp, scale=-0.5), inc="act")
        ln_state["act"] = t_ex
        return b.emit("dve", lambda v: v.scalar_tensor_tensor(
            a2, m, -1.0, rstd, ALU.mult, ALU.mult), waits=[t_ex], inc="dve")

    def ln_bcast(ro, tok_rows, ps_r, ps_a):
        tr = b.emit("pe", lambda t: t.matmul(
            ps_r[:, :], ones_row_f32[0:1, 0:128], rows[0:1, ro:ro + 512],
            start=True, stop=True), waits=[tok_rows], inc="pe")
        ta = b.emit("pe", lambda t: t.matmul(
            ps_a[:, :], ones_row_f32[0:1, 0:128],
            rows[0:1, ro + 512:ro + 1024], start=True, stop=True),
            waits=[tok_rows], inc="pe")
        return tr, ta

    ps45 = {0: None, 1: None}
    uw = [None, None]
    t_prw_war = [{0: None, 1: None}, {0: None, 1: None}]
    t_aload = {0: None, 1: None}
    t_a_war = None
    t_last_exp = None
    t_negM_evac = None
    t_recip_mult = None
    t_rcm = [None]
    t_attn_read = None
    t_krw = {}
    t_y1_pass3 = [None, None]
    t_y1_store = [None] * NCH
    t_osq_prev = None
    t_prev_statq = None

    for c in range(NCH):
        xb = x_sb[c % 2]
        slot = c % RING
        t_x = b.emit("pool", lambda g, c=c, xb=xb: g.dma_start(
            xb[:], xt.rearrange("(dc p) t -> p dc t", p=128)
            [:, :, c * CW:(c + 1) * CW]),
            waits=[t_y1_pass3[c % 2]], inc="ld")
        extra0 = [t_sb1c, t_rb1c] if c == 0 else [t_y1_pass3[(c - 1) % 2]]
        t_sqx, t_stx = ln_stats(
            lambda dc, xb=xb: xb[:] if dc is None else xb[:, dc * CW:(dc + 1) * CW],
            CW, t_x, P6, P7, t_prev_statq, extra=extra0)
        t_rowsx = ln_rows(P6, P7, t_stx, RO_X)
        t_R, t_A2 = ln_bcast(RO_X, t_rowsx, P6, P7)
        t_h = None
        for dc in range(ND):
            b.emit("dve", lambda v, dc=dc, xb=xb: v.tensor_mul(
                tmp0, xb[:, dc * CW:(dc + 1) * CW], P6[:, :]),
                waits=[t_R, t_A2, t_osq_prev] if dc == 0 else [])
            t_h = b.emit("dve", lambda v, dc=dc: v.tensor_add(
                h_sb[:, dc * CW:(dc + 1) * CW], tmp0, P7[:, :]),
                inc="dve" if dc == ND - 1 else None)

        def proj_fm(wT_sb, t_w, bias_row, out_cb):
            toks = []
            for dco in range(ND):
                pi = dco % 2
                ps = PP45[pi]
                for dci in range(ND):
                    b.emit("pe", lambda t, dco=dco, dci=dci, ps=ps: t.matmul(
                        ps[:, :],
                        wT_sb[:, dci * D + dco * 128:dci * D + (dco + 1) * 128],
                        h_sb[:, dci * CW:(dci + 1) * CW],
                        start=(dci == 0), stop=False),
                        waits=[t_h, t_w, ps45[pi]] if dci == 0 else [])
                tm = b.emit("pe", lambda t, dco=dco, ps=ps: t.matmul(
                    ps[:, :], bias_row[0:1, dco * 128:(dco + 1) * 128],
                    ones_row_bf[0:1, 0:512], start=False, stop=True),
                    inc="pe")
                last = b.emit("dve", lambda v, dco=dco, ps=ps: v.tensor_copy(
                    out_cb(dco), ps[:, :]), waits=[tm], inc="dve")
                ps45[pi] = last
                toks.append(last)
            return toks

        q_toks = proj_fm(wqT, t_wqT, bq_sb,
                         lambda dco: q_sb[:, dco * CW:(dco + 1) * CW])
        t_q = q_toks[-1]
        q_sq = []
        for dcq in range(ND):
            q_sq.append(b.emit("act", lambda s, dcq=dcq: s.activation(
                sq_sb[:, dcq * CW:(dcq + 1) * CW],
                q_sb[:, dcq * CW:(dcq + 1) * CW], AF.Square),
                waits=[q_toks[dcq]] +
                ([t_prev_statq, t_stx] if dcq == 0 else []),
                inc="act"))
        k_toks = proj_fm(wkT, t_wkT, bk_sb,
                         lambda dco, slot=slot: kr[:, dco * RING * CW + slot * CW:
                                                   dco * RING * CW + (slot + 1) * CW])
        t_k = k_toks[-1]
        t_krw[slot] = t_k
        t_v = None
        for tt in range(4):
            for fh in range(2):
                ps = PP45[fh]
                for dci in range(ND):
                    b.emit("pe", lambda t, tt=tt, fh=fh, dci=dci, ps=ps:
                           t.matmul(ps[:, :],
                                    h_sb[:, dci * CW + tt * 128:
                                         dci * CW + (tt + 1) * 128],
                                    wvT[:, dci * D + fh * 512:
                                        dci * D + (fh + 1) * 512],
                                    start=(dci == 0), stop=False),
                           waits=[t_h, t_wvT, t_vones, ps45[fh]]
                           if dci == 0 else [])
                tm = b.emit("pe", lambda t, fh=fh, ps=ps: t.matmul(
                    ps[:, :], ones_row_bf[0:1, 0:128],
                    bv_sb[0:1, fh * 512:(fh + 1) * 512],
                    start=False, stop=True), inc="pe")
                vslot = (slot * 4 + tt) * NH * VST
                t_v = b.emit("dve", lambda v, fh=fh, ps=ps, vs=vslot:
                             v.tensor_copy(
                                 vr[:, vs + fh * 8 * VST:vs + (fh + 1) * 8 * VST]
                                 .rearrange("p (g e) -> p g e", e=VST)
                                 [:, :, 0:64],
                                 ps.rearrange("p (g e) -> p g e", e=64)),
                             waits=[tm], inc="dve")
                ps45[fh] = t_v

        nslots = [PSA[0:1, 0:512], PSA[0:1, 512:1024],
                  PSB[0:1, 0:512], PSB[0:1, 512:1024]]
        ntoks = [None] * 4
        t_qn_done = [None] * ND
        for p_ in range(NH // 2):
            s0, s1 = (2 * p_) % 4, (2 * p_) % 4 + 1
            tq = b.emit("pe", lambda t, dc=p_, ns=nslots[s0]: t.matmul(
                ns, ones_col_bf[0:64, :],
                sq_sb[0:64, dc * CW:(dc + 1) * CW],
                start=True, stop=True),
                waits=[q_sq[p_], ntoks[s0], ntoks[s1]] +
                ([uw[0], uw[1]] if p_ == 0 else []))
            tq = b.emit("pe", lambda t, dc=p_, ns=nslots[s1]: t.matmul(
                ns, ones_col_bf[64:128, :],
                sq_sb[64:128, dc * CW:(dc + 1) * CW],
                start=True, stop=True), inc="pe")
            ntoks[s0] = b.emit(
                "dve", lambda v, h=2 * p_, ns=nslots[s0]: v.reduce_max(
                    rows[0:1, NQ + h:NQ + h + 1], ns, axis=AX.X),
                waits=[tq], inc="dve")
            ntoks[s1] = b.emit(
                "dve", lambda v, h=2 * p_ + 1, ns=nslots[s1]: v.reduce_max(
                    rows[0:1, NQ + h:NQ + h + 1], ns, axis=AX.X),
                inc="dve")
            t_qn_done[p_] = ntoks[s1]
        k_sq = []
        for dck in range(ND):
            kcolb = dck * RING * CW + slot * CW
            k_sq.append(b.emit("act", lambda s, dck=dck, kcolb=kcolb:
                               s.activation(
                                   sq_sb[:, dck * CW:(dck + 1) * CW],
                                   kr[:, kcolb:kcolb + CW], AF.Square),
                               waits=[k_toks[dck], q_sq[dck],
                                      t_qn_done[dck]], inc="act"))
        nrm = None
        for p_ in range(NH // 2):
            s0, s1 = (2 * p_) % 4, (2 * p_) % 4 + 1
            tq = b.emit("pe", lambda t, dc=p_, ns=nslots[s0]: t.matmul(
                ns, ones_col_bf[0:64, :],
                sq_sb[0:64, dc * CW:(dc + 1) * CW],
                start=True, stop=True),
                waits=[k_sq[p_], ntoks[s0], ntoks[s1]])
            tq = b.emit("pe", lambda t, dc=p_, ns=nslots[s1]: t.matmul(
                ns, ones_col_bf[64:128, :],
                sq_sb[64:128, dc * CW:(dc + 1) * CW],
                start=True, stop=True), inc="pe")
            ntoks[s0] = b.emit(
                "dve", lambda v, h=2 * p_, slot=slot, ns=nslots[s0]:
                v.reduce_max(
                    rows[0:1, NK + slot * 16 + h:NK + slot * 16 + h + 1],
                    ns, axis=AX.X), waits=[tq], inc="dve")
            nrm = ntoks[s1] = b.emit(
                "dve", lambda v, h=2 * p_ + 1, slot=slot, ns=nslots[s1]:
                v.reduce_max(
                    rows[0:1, NK + slot * 16 + h:NK + slot * 16 + h + 1],
                    ns, axis=AX.X), inc="dve")
        uw[0] = uw[1] = nrm
        slots_present = [s for s in range(RING) if s <= c]
        b.emit("dve", lambda v, s0=slots_present[0]: v.tensor_copy(
            rows[0:1, KC:KC + 16], rows[0:1, NK + s0 * 16:NK + s0 * 16 + 16]),
            waits=[nrm])
        for s in slots_present[1:]:
            b.emit("dve", lambda v, s=s: v.tensor_max(
                rows[0:1, KC:KC + 16], rows[0:1, KC:KC + 16],
                rows[0:1, NK + s * 16:NK + s * 16 + 16]))
        b.emit("dve", lambda v: v.tensor_mul(
            rows[0:1, MM_:MM_ + 16], rows[0:1, NQ:NQ + 16],
            rows[0:1, KC:KC + 16]))
        t_m1 = b.emit("dve", lambda v: v.tensor_single_scalar(
            rows[0:1, MM_:MM_ + 16], rows[0:1, MM_:MM_ + 16], 1e-30, ALU.max),
            inc="dve")
        b.emit("act", lambda s: s.activation(
            rows[0:1, MM_:MM_ + 16], rows[0:1, MM_:MM_ + 16], AF.Ln),
            waits=[t_m1])
        t_m2 = b.emit("act", lambda s: s.activation(
            rows[0:1, MM_:MM_ + 16], rows[0:1, MM_:MM_ + 16], AF.Exp,
            scale=0.5), inc="act")
        t_m3 = b.emit("dve", lambda v: v.tensor_single_scalar(
            rows[0:1, MM_:MM_ + 16], rows[0:1, MM_:MM_ + 16], -0.125,
            ALU.mult), waits=[t_m2], inc="dve")
        tmm = b.emit("pe", lambda t: t.matmul(
            P7[0:128, 0:NH], ones_row_f32[0:1, 0:128],
            rows[0:1, MM_:MM_ + 16], start=True, stop=True),
            waits=[t_m3, t_rcm[0]], inc="pe")
        t_negM_evac = b.emit("dve", lambda v: v.tensor_copy(
            negM[:], P7[0:128, 0:NH]), waits=[tmm, t_last_exp], inc="dve")

        kts = list(range(max(0, 4 * c - 8), 4 * c + 4))
        nkt = len(kts)
        nwav = nkt // 2
        if c == 0:
            t_aload[0] = b.emit("sync", lambda s: s.dma_start(
                a_buf[0][:], A_dram[0]), waits=[t_abuild_done], inc="hw")
            t_aload[1] = b.emit("sync", lambda s: s.dma_start(
                a_buf[1][:], A_dram[1]), waits=[t_abuild_done], inc="hw")
        pending_end = [None]
        for p in range(NH // 2):
            dc = p
            SPv = [PSA, PSB]
            t_avlast = [None, None]
            t_amlast = [None, None]
            pend = None

            def emit_avs(item, dc=dc):
                w, am_toks, vcols = item
                half = w % 2
                for hh in range(2):
                    for j in range(2):
                        idx = 2 * w + j
                        prq = prW[hh][:, half * 1024 + j * 512:
                                      half * 1024 + j * 512 + 512]
                        t_av = b.emit(
                            "pe", lambda t, vcol=vcols[2 * hh + j], prq=prq,
                            hh=hh, s=(idx == 0), e=(idx == nkt - 1): t.matmul(
                                PP45[hh][0:65, :], vr[:, vcol:vcol + VST],
                                prq, start=s, stop=e),
                            waits=[am_toks[2 * hh + j]] +
                            ([ps45[hh]] if idx == 0 else []),
                            inc="pe")
                        t_avlast[hh] = t_av
                    t_prw_war[hh][half] = t_avlast[hh]

            for w in range(nwav):
                half = w % 2
                mm_tok = [None, None]
                vcols = [0, 0, 0, 0]
                for hh in range(2):
                    pb = 64 * hh
                    for j in range(2):
                        kt = kts[2 * w + j]
                        cs = kt // 4
                        sl = cs % RING
                        jj = kt % 4
                        kcol = dc * RING * CW + sl * CW + jj * 128
                        vcols[2 * hh + j] = (sl * 4 + jj) * NH * VST + \
                            (2 * p + hh) * VST
                        mm_tok[hh] = b.emit(
                            "pe", lambda t, kcol=kcol, pb=pb, dc=dc, hh=hh,
                            j=j: t.matmul(
                                SPv[hh][:, j * 512:j * 512 + 512],
                                kr[pb:pb + 64, kcol:kcol + 128],
                                q_sb[pb:pb + 64, dc * CW:(dc + 1) * CW],
                                start=True, stop=True),
                            waits=([uw[hh]] +
                                   ([t_q, t_v] if hh == 0 else []))
                            if j == 0 else [],
                            inc="pe" if j == 1 else None)
                if w == 0 and pending_end[0] is not None:
                    pending_end[0]()
                    pending_end[0] = None
                if pend is not None:
                    emit_avs(pend)
                am_toks = [None] * 4
                for hh in range(2):
                    h = 2 * p + hh
                    t_ex = b.emit(
                        "act", lambda s, hh=hh, half=half, h=h: s.activation(
                            prW[hh][:, half * 1024:half * 1024 + 1024],
                            SPv[hh][:, :], AF.Exp,
                            bias=negM[:, h:h + 1], scale=0.125),
                        waits=[mm_tok[hh], t_negM_evac,
                               t_prw_war[hh][half]],
                        inc="act")
                    uw[hh] = t_ex
                    t_last_exp = t_ex
                    for j in range(2):
                        kt = kts[2 * w + j]
                        a0 = 384 - (kt * 128 - c * CW)
                        prq = prW[hh][:, half * 1024 + j * 512:
                                      half * 1024 + j * 512 + 512]
                        t_am = b.emit(
                            "dve", lambda v, prq=prq, hh=hh, a0=a0:
                            v.tensor_mul(prq, prq,
                                         a_buf[hh][:, a0:a0 + 512]),
                            waits=[t_ex, t_aload[hh]], inc="dve")
                        am_toks[2 * hh + j] = t_am
                        t_amlast[hh] = t_am
                pend = (w, am_toks, vcols)
            emit_avs(pend)
            for hh in range(2):
                nxt = 2 * p + 2 + hh
                if nxt < NH:
                    t_aload[hh] = b.emit(
                        "sync", lambda s, hh=hh, nxt=nxt: s.dma_start(
                            a_buf[hh][:], A_dram[nxt]),
                        waits=[t_amlast[hh]], inc="hw")
                elif c + 1 < NCH:
                    t_aload[hh] = b.emit(
                        "sync", lambda s, hh=hh: s.dma_start(
                            a_buf[hh][:], A_dram[hh]),
                        waits=[t_amlast[hh]], inc="hw")
            def pair_end(dc=dc, avl=tuple(t_avlast)):
                global_recip = None
                for hh in range(2):
                    pb = 64 * hh
                    b.emit("act", lambda s, hh=hh, avl=avl: s.activation(
                        rr_sb, PP45[hh][64:65, :], AF.Ln),
                        waits=[avl[hh], t_rcm[0]])
                    t_rc = b.emit("act", lambda s: s.activation(
                        rr_bf, rr_sb, AF.Exp, scale=-1.0), inc="act")
                    t_rm = b.emit("pe", lambda t: t.matmul(
                        P7[0:64, :], ones_row_bf[0:1, 0:64], rr_bf,
                        start=True, stop=True),
                        waits=[t_rc, t_rcm[0]], inc="pe")
                    t_ev = b.emit("dve", lambda v, hh=hh, avl=avl:
                                  v.tensor_copy(avraw, PP45[hh][0:64, :]),
                                  waits=[avl[hh], t_rcm[0]], inc="dve")
                    ps45[hh] = t_ev
                    t_rcm[0] = b.emit(
                        "dve", lambda v, dc=dc, pb=pb: v.tensor_tensor(
                            attn_sb[pb:pb + 64, dc * CW:(dc + 1) * CW],
                            avraw, P7[0:64, :], ALU.mult),
                        waits=[t_rm, t_ev, t_attn_read], inc="dve")
            pending_end[0] = pair_end
        pending_end[0]()
        pending_end[0] = None
        t_recip_mult = t_rcm[0]

        t_o = None
        t_sqo = None
        t_sto = None
        for dco in range(ND):
            pi = dco % 2
            ps = PP45[pi]
            for dci in range(ND):
                b.emit("pe", lambda t, dco=dco, dci=dci, ps=ps: t.matmul(
                    ps[:, :],
                    woT[:, dci * D + dco * 128:dci * D + (dco + 1) * 128],
                    attn_sb[:, dci * CW:(dci + 1) * CW],
                    start=(dci == 0), stop=False),
                    waits=[t_recip_mult, t_woT, ps45[pi]] if dci == 0 else [])
            tm = b.emit("pe", lambda t, dco=dco, ps=ps: t.matmul(
                ps[:, :], bo_sb[0:1, dco * 128:(dco + 1) * 128],
                ones_row_bf[0:1, 0:512], start=False, stop=True), inc="pe")
            t_o = b.emit("dve", lambda v, dco=dco, ps=ps: v.tensor_copy(
                h_sb[:, dco * CW:(dco + 1) * CW], ps[:, :]),
                waits=[tm], inc="dve")
            ps45[pi] = t_o
            t_sqo = b.emit("act", lambda s, dco=dco: s.activation(
                sq_sb[:, dco * CW:(dco + 1) * CW],
                h_sb[:, dco * CW:(dco + 1) * CW], AF.Square),
                waits=[t_o] + ([t_recip_mult] if dco == 0 else []),
                inc="act")
            b.emit("pe", lambda t, dco=dco: t.matmul(
                P6[0:1, :], ones_col_bf[:],
                h_sb[:, dco * CW:(dco + 1) * CW],
                start=(dco == 0), stop=(dco == ND - 1)),
                waits=[t_o])
            t_sto = b.emit("pe", lambda t, dco=dco: t.matmul(
                P7[0:1, :], ones_col_bf[:],
                sq_sb[:, dco * CW:(dco + 1) * CW],
                start=(dco == 0), stop=(dco == ND - 1)),
                waits=[t_sqo], inc="pe" if dco == ND - 1 else None)

        t_osq_prev = t_sqo
        t_prev_statq = t_sto
        t_rowso = ln_rows(P6, P7, t_sto, RO_O)
        t_Ro, t_A2o = ln_bcast(RO_O, t_rowso, P6, P7)
        t_p3 = None
        for dc in range(ND):
            b.emit("dve", lambda v, dc=dc: v.tensor_mul(
                tmp0, h_sb[:, dc * CW:(dc + 1) * CW], P6[:, :]),
                waits=[t_Ro, t_A2o,
                       t_y1_store[c - 1] if c > 0 else None]
                if dc == 0 else [])
            b.emit("dve", lambda v: v.tensor_add(tmp1, tmp0, P7[:, :]))
            t_p3 = b.emit("dve", lambda v, dc=dc, xb=xb: v.tensor_add(
                y1c[:, dc * CW:(dc + 1) * CW], tmp1,
                xb[:, dc * CW:(dc + 1) * CW]),
                inc="dve" if dc == ND - 1 else None)
        t_y1_pass3[c % 2] = t_p3
        t_y1_store[c] = b.emit("sync", lambda s, c=c: s.dma_start(
            y1_dram.rearrange("(dc p) t -> p dc t", p=128)
            [:, :, c * CW:(c + 1) * CW], y1c[:]),
            waits=[t_p3], inc="hw")
        t_attn_read = t_y1_store[c]

    b.emit("pe", lambda t: t.matmul(
        P6[0:1, 0:8], ones_row_f32[0:1, 0:1], ones_row_f32[0:1, 0:8],
        start=True, stop=True),
        waits=[t_y1_pass3[(NCH - 1) % 2]], inc="ph")
    b.emit("act", lambda s: s.copy(dummy[0:1, 0:2], ones_row_f32[0:1, 0:2]),
           inc="ph")
    b.emit("dve", lambda v: v.memset(dummy[0:1, 2:4], 0.0), inc="ph")
    b.emit("pool", lambda g: g.memset(dummy[0:1, 4:6], 0.0), inc="ph")
    att.close()
    BARRIER = ("ph", 4)

    ffn = contextlib.ExitStack()
    y1_sb = [ffn.enter_context(nc.sbuf_tensor(f"y1_{i}", [128, ND * CW], BF16))
             for i in range(2)]
    h2_sb = ffn.enter_context(nc.sbuf_tensor("h2_sb", [128, ND * CW], BF16))
    h1_sb = ffn.enter_context(nc.sbuf_tensor("h1_sb", [128, NHC * CW], BF16))
    h1w_sb = ffn.enter_context(nc.sbuf_tensor("h1w", [128, NHC * CW], BF16))
    mix_sb = ffn.enter_context(nc.sbuf_tensor("mix", [128, ND * CW], BF16))
    of_sb = [ffn.enter_context(nc.sbuf_tensor(f"of{i}", [128, 512], F32))
             for i in range(2)]
    Wall = ffn.enter_context(nc.sbuf_tensor("Wall", [128, NE * CW], BF16))
    wrows = ffn.enter_context(nc.sbuf_tensor("wrows", [1, NE * CW], BF16))
    prob_sb = ffn.enter_context(nc.sbuf_tensor("prob", [128, 4 * NE], F32))
    top8 = ffn.enter_context(nc.sbuf_tensor("top8", [128, NE], F32))
    wtm = ffn.enter_context(nc.sbuf_tensor("wtm", [128, 4 * NE], F32))
    rcol = ffn.enter_context(nc.sbuf_tensor("rcol", [128, 8], F32))
    row2 = ffn.enter_context(nc.sbuf_tensor("row2", [1, 1024], F32))
    w1b = [ffn.enter_context(nc.sbuf_tensor(f"w1b{i}", [128, ND * 1024], BF16))
           for i in range(2)]
    w2b = [ffn.enter_context(nc.sbuf_tensor(f"w2b{i}", [128, 8 * D], BF16))
           for i in range(2)]
    rb_sb = ffn.enter_context(nc.sbuf_tensor("rb_sb", [1, NE], BF16))
    sb2_sb = ffn.enter_context(nc.sbuf_tensor("sb2_sb", [1, D], BF16))
    rb2_sb = ffn.enter_context(nc.sbuf_tensor("rb2_sb", [1, NE * D], BF16))
    rw_sb = ffn.enter_context(nc.sbuf_tensor("rw_sb", [128, ND * NE], BF16))
    for nm, srcT, dst in (("rb", rbt, rb_sb), ("sb2", sb2, sb2_sb),
                          ("rb2", rb2, rb2_sb)):
        t_bias[nm] = b.emit("pool", lambda g, a=srcT, d=dst: g.dma_start(
            d[:], a), waits=[BARRIER], inc="ld")
    t_rw = b.emit("pool", lambda g: g.dma_start(
        rw_sb[:], rwt.rearrange("(dc p) e -> p dc e", p=128)),
        waits=[BARRIER], inc="ld")

    def load_w1(eidx, half, waits):
        src = sw1_bf if eidx < 0 else rw1_bf[eidx]
        tok = tcast[4] if eidx < 0 else t_rw1c[eidx]
        return b.emit("sync", lambda s, src=src, half=half: s.dma_start(
            w1b[half][:],
            src.rearrange("(dc p) h1 -> p dc h1", p=128)
            [:, :, half * 1024:(half + 1) * 1024]),
            waits=list(waits) + [tok, BARRIER], inc="hw")

    def load_w2(eidx, half, waits):
        src = sw2_bf if eidx < 0 else rw2_bf[eidx]
        tok = tcast[5] if eidx < 0 else t_rw2c[eidx]
        return b.emit("sync", lambda s, src=src, half=half: s.dma_start(
            w2b[half][:],
            src.rearrange("(hc p) f -> p hc f", p=128)
            [:, half * 8:(half + 1) * 8, :]),
            waits=list(waits) + [tok, BARRIER], inc="hw")

    t_pool_tok = {}
    t_h1_done = {}
    t_h1_read = {}
    t_mixadd = {}
    t_final = [None, None]
    t_of_war = {0: None, 1: None}
    t_h1ps_war = {0: None, 1: None}
    t_p0_war = []
    t_p1_war = None
    t_wtm_war = None
    t_mix_sq = None
    t_wall_pool_read = None
    t_w1_load = {}
    t_w2_load = {}
    t_sq_war = t_prev_statq
    t_final_prev = None
    t_h2 = None

    t_w1_load[(0, -1)] = (load_w1(-1, 0, []), load_w1(-1, 1, []))
    t_w2_load[(0, -1)] = (load_w2(-1, 0, []), load_w2(-1, 1, []))

    t_y1ld = {}
    t_y1ld[0] = b.emit("sync", lambda s: s.dma_start(
        y1_sb[0][:], y1_dram.rearrange("(dc p) t -> p dc t", p=128)
        [:, :, 0:CW]), waits=[t_y1_store[0], BARRIER], inc="hw")
    hst = {"sq_war": t_sq_war, "h2": None, "R2": {}, "A22": {},
           "sumsq": {}}

    def h2_stats(tcx):
        yb = y1_sb[tcx % 2]
        t_sq = b.emit("act", lambda s, yb=yb: s.activation(
            sq_sb[:], yb[:], AF.Square),
            waits=[t_y1ld[tcx], hst["sq_war"], hst["h2"]], inc="act")
        tok = None
        for dc in range(ND):
            b.emit("pe", lambda t, dc=dc, yb=yb: t.matmul(
                P6[0:1, :], ones_col_bf[:], yb[:, dc * CW:(dc + 1) * CW],
                start=(dc == 0), stop=(dc == ND - 1)),
                waits=[t_y1ld[tcx], hst["h2"]] if dc == 0 else [])
        for dc in range(ND):
            tok = b.emit("pe", lambda t, dc=dc: t.matmul(
                P7[0:1, :], ones_col_bf[:], sq_sb[:, dc * CW:(dc + 1) * CW],
                start=(dc == 0), stop=(dc == ND - 1)),
                waits=[t_sq] if dc == 0 else [],
                inc="pe" if dc == ND - 1 else None)
        hst["sumsq"][tcx] = tok
        t_rows2 = ln_rows(P6, P7, tok, RO_X)
        t_R2, t_A22 = ln_bcast(RO_X, t_rows2, P6, P7)
        hst["R2"][tcx] = t_R2
        hst["A22"][tcx] = t_A22

    h2_stats(0)
    for tc in range(NCH):
        yb = y1_sb[tc % 2]
        t_y1 = t_y1ld[tc]
        t_R2 = hst["R2"][tc]
        t_A22 = hst["A22"][tc]
        t_h2 = None
        war_h2 = t_h1_done.get((tc - 1, NE - 1))
        for dc in range(ND):
            b.emit("dve", lambda v, dc=dc, yb=yb: v.tensor_mul(
                tmp0, yb[:, dc * CW:(dc + 1) * CW], P6[:, :]),
                waits=[t_R2, t_A22, war_h2] if dc == 0 else [])
            t_h2 = b.emit("dve", lambda v, dc=dc: v.tensor_add(
                h2_sb[:, dc * CW:(dc + 1) * CW], tmp0, P7[:, :]),
                inc="dve" if dc == ND - 1 else None)
        hst["h2"] = t_h2

        t_rmm = None
        for tt in range(4):
            for dci in range(ND):
                b.emit("pe", lambda t, tt=tt, dci=dci,
                       s=(tt == 0 and dci == 0): t.matmul(
                           PSA[:, tt * 8:(tt + 1) * 8],
                           h2_sb[:, dci * CW + tt * 128:dci * CW + (tt + 1) * 128],
                           rw_sb[:, dci * 8:(dci + 1) * 8],
                           start=s, stop=False),
                       waits=[t_h2, t_rw, t_bias['rb'], t_h1ps_war[0]] + t_p0_war
                       if (tt == 0 and dci == 0) else [])
            t_rmm = b.emit("pe", lambda t, tt=tt: t.matmul(
                PSA[:, tt * 8:(tt + 1) * 8],
                ones_row_bf[0:1, tt * 128:(tt + 1) * 128],
                rb_sb[:], start=False, stop=(tt == 3)),
                inc="pe" if tt == 3 else None)
        t_cp0 = b.emit("dve", lambda v: v.tensor_copy(
            wtm[:], PSA[:, 0:32]), waits=[t_rmm, t_wtm_war],
            inc="dve")
        t_w_tm = None
        t_e = None
        t_red = None
        for tt in range(4):
            sl = slice(tt * 8, (tt + 1) * 8)
            t_red = b.emit("dve", lambda v, sl=sl: v.reduce_max(
                top8[:, 0:1], wtm[:, sl], axis=AX.X, negate=True),
                waits=[t_cp0, t_e], inc="dve")
            t_e = b.emit("act", lambda s, sl=sl: s.activation(
                prob_sb[:, sl], wtm[:, sl], AF.Exp, bias=top8[:, 0:1]),
                waits=[t_red], inc="act")
            b.emit("dve", lambda v, sl=sl, tt=tt: v.reduce_sum(
                rcol[:, tt:tt + 1], prob_sb[:, sl], axis=AX.X), waits=[t_e])
            t_mx = b.emit("dve", lambda v, sl=sl: v.max(
                out=top8[:], in_=prob_sb[:, sl]), inc="dve")
            t_cp = b.emit("dve", lambda v, tt=tt: v.tensor_copy(
                rcol[:, 4 + tt:5 + tt], top8[:, 1:2]),
                waits=[t_mx], force=True, inc="dve")
            t_ge = b.emit("dve", lambda v, sl=sl, tt=tt: v.tensor_single_scalar(
                wtm[:, sl], prob_sb[:, sl], rcol[:, 4 + tt:5 + tt], ALU.is_ge),
                waits=[t_cp], force=True, inc="dve")
            t_w_tm = b.emit("dve", lambda v, sl=sl: v.tensor_mul(
                wtm[:, sl], wtm[:, sl], prob_sb[:, sl]),
                waits=[t_ge], force=True, inc="dve")
        t_p0_war = [t_cp0]
        for ei in range(-1, NE):
            lw1 = t_w1_load[(tc, ei)]
            t_last = None
            for hc in range(NHC):
                half = hc // 8
                ps = PSA[:, (hc % 2) * 512:(hc % 2) * 512 + 512]
                for dci in range(ND):
                    col = half * 0 + (hc % 8) * 128 + dci * 1024
                    b.emit("pe", lambda t, col=col, dci=dci, half=half, ps=ps:
                           t.matmul(ps, w1b[half][:, col:col + 128],
                                    h2_sb[:, dci * CW:(dci + 1) * CW],
                                    start=(dci == 0), stop=(dci == ND - 1)),
                           waits=([lw1[half], t_h2, t_h1ps_war[hc % 2],
                                   t_sb1c, t_rb1c] +
                                  (t_p0_war if hc < 2 else []))
                           if dci == 0 else [],
                           inc="pe" if dci == ND - 1 else None)
                tok_mm = ("pe", b.sems["pe"][1])
                bc = (sb1c[:, hc:hc + 1] if ei < 0
                      else rb1c[:, ei * NHC + hc:ei * NHC + hc + 1])
                dst = h1w_sb if ei < 0 else h1_sb
                if ei < 0:
                    war_dst = t_h1_read.get((tc - 1, NE - 1))
                elif ei == 0:
                    war_dst = t_pool_tok.get((tc - 1, NE - 1, hc))
                else:
                    war_dst = t_pool_tok.get((tc, ei - 1, hc))
                t_g = b.emit("act", lambda s, hc=hc, ps=ps, bc=bc, dst=dst:
                             s.activation(dst[:, hc * CW:(hc + 1) * CW],
                                          ps, AF.Gelu, bias=bc),
                             waits=[tok_mm, war_dst], inc="act")
                t_h1ps_war[hc % 2] = t_g
                if ei >= 0:
                    t_p = b.emit("pool", lambda g, hc=hc, ei=ei: g.tensor_mul(
                        h1w_sb[:, hc * CW:(hc + 1) * CW],
                        h1_sb[:, hc * CW:(hc + 1) * CW],
                        Wall[:, ei * CW:(ei + 1) * CW]),
                        waits=[t_g, t_wall, t_h1_read.get((tc, ei - 1))],
                        inc="pool")
                else:
                    t_p = t_g
                t_pool_tok[(tc, ei, hc)] = t_p
                t_last = t_p
            t_h1_done[(tc, ei)] = t_last
            if ei == -1:
                if tc + 1 < NCH:
                    t_y1ld[tc + 1] = b.emit(
                        "sync", lambda s, tc=tc: s.dma_start(
                            y1_sb[(tc + 1) % 2][:],
                            y1_dram.rearrange("(dc p) t -> p dc t", p=128)
                            [:, :, (tc + 1) * CW:(tc + 2) * CW]),
                        waits=[t_y1_store[tc + 1], t_final[(tc + 1) % 2]],
                        inc="hw")
            t_wall_pool_read = t_last if ei >= 0 else t_wall_pool_read
            if ei + 1 < NE:
                t_w1_load[(tc, ei + 1)] = (
                    load_w1(ei + 1, 0, [t_h1_done[(tc, ei)]]),
                    load_w1(ei + 1, 1, []))
            elif tc + 1 < NCH:
                t_w1_load[(tc + 1, -1)] = (
                    load_w1(-1, 0, [t_h1_done[(tc, ei)]]),
                    load_w1(-1, 1, []))
            lw2 = t_w2_load[(tc, ei)]
            for fc in range(ND):
                pi = fc % 2
                ps = PP45[pi]
                for hc in range(NHC):
                    half = hc // 8
                    col = (hc % 8) * D + fc * 128
                    b.emit("pe", lambda t, hc=hc, col=col, half=half, ps=ps:
                           t.matmul(ps[:, :], w2b[half][:, col:col + 128],
                                    h1w_sb[:, hc * CW:(hc + 1) * CW],
                                    start=(hc == 0), stop=False),
                           waits=[lw2[half], t_pool_tok[(tc, ei, hc)]] +
                           ([ps45[pi]] if hc == 0 else []))
                if ei < 0:
                    tmb = b.emit("pe", lambda t, fc=fc, ps=ps: t.matmul(
                        ps[:, :], sb2_sb[0:1, fc * 128:(fc + 1) * 128],
                        ones_row_bf[0:1, 0:512], start=False, stop=True),
                        waits=[t_bias['sb2']], inc="pe")
                else:
                    tmb = b.emit("pe", lambda t, fc=fc, ps=ps, ei=ei: t.matmul(
                        ps[:, :],
                        rb2_sb[0:1, ei * D + fc * 128:ei * D + (fc + 1) * 128],
                        wrows[0:1, ei * CW:(ei + 1) * CW],
                        start=False, stop=True),
                        waits=[t_bias['rb2'], t_wall], inc="pe")
                if ei < 0:
                    t_ma = b.emit("dve", lambda v, fc=fc, ps=ps: v.tensor_copy(
                        mix_sb[:, fc * CW:(fc + 1) * CW], ps[:, :]),
                        waits=[tmb, t_mix_sq], inc="dve")
                else:
                    t_ma = b.emit("dve", lambda v, fc=fc, ps=ps: v.tensor_add(
                        mix_sb[:, fc * CW:(fc + 1) * CW],
                        mix_sb[:, fc * CW:(fc + 1) * CW], ps[:, :]),
                        waits=[tmb], inc="dve")
                t_mixadd[(tc, ei, fc)] = t_ma
                ps45[pi] = t_ma
                if ei == NE - 1:
                    t_sqm = b.emit("act", lambda s, fc=fc: s.activation(
                        sq_sb[:, fc * CW:(fc + 1) * CW],
                        mix_sb[:, fc * CW:(fc + 1) * CW], AF.Square),
                        waits=[t_ma] +
                        ([t_R2, hst["sumsq"].get(tc + 1)] if fc == 0 else []),
                        inc="act")
                    b.emit("pe", lambda t, fc=fc: t.matmul(
                        PSB[0:1, 512:1024], ones_col_bf[:],
                        mix_sb[:, fc * CW:(fc + 1) * CW],
                        start=(fc == 0), stop=(fc == ND - 1)),
                        waits=[t_ma] + ([t_final_prev] if fc == 0 else []))
                    t_stm = b.emit("pe", lambda t, fc=fc: t.matmul(
                        PSB[0:1, 0:512], ones_col_bf[:],
                        sq_sb[:, fc * CW:(fc + 1) * CW],
                        start=(fc == 0), stop=(fc == ND - 1)),
                        waits=[t_sqm] + ([t_wall] if fc == 0 else []),
                        inc="pe" if fc == ND - 1 else None)
            t_h1_read[(tc, ei)] = t_mixadd[(tc, ei, ND - 1)]
            if ei == -1:
                ttok = None
                for tt in range(4):
                    ttok = b.emit("pe", lambda t, tt=tt: t.transpose(
                        PSB[0:1, tt * 128:(tt + 1) * 128],
                        rcol[:, tt:tt + 1], eye_f32[:]),
                        waits=[t_w_tm, t_eye, t_p1_war] if tt == 0 else [],
                        inc="pe" if tt == 3 else None)
                t_sr = b.emit("dve", lambda v: v.tensor_copy(
                    row2[0:1, 0:512], PSB[0:1, 0:512]), waits=[ttok], inc="dve")
                b.emit("act", lambda s: s.activation(
                    row2[0:1, 512:1024], row2[0:1, 0:512], AF.Ln), waits=[t_sr])
                t_rcp = b.emit("act", lambda s: s.activation(
                    row2[0:1, 512:1024], row2[0:1, 512:1024], AF.Exp, scale=-1.0),
                    inc="act")
                t_prev = t_sr
                for e in range(NE):
                    tt_tok = None
                    for tt in range(4):
                        tt_tok = b.emit("pe", lambda t, e=e, tt=tt: t.transpose(
                            PSB[0:1, tt * 128:(tt + 1) * 128],
                            wtm[:, tt * 8 + e:tt * 8 + e + 1],
                            eye_f32[:]),
                            waits=[t_w_tm, t_prev] if tt == 0 else [],
                            inc="pe" if tt == 3 else None)
                    t_prev = b.emit("dve", lambda v, e=e: v.tensor_copy(
                        wrows[0:1, e * CW:(e + 1) * CW], PSB[0:1, 0:512]),
                        waits=[tt_tok], inc="dve")
                t_wtm_war = tt_tok
                t_wn = None
                for e in range(NE):
                    t_wn = b.emit("dve", lambda v, e=e: v.tensor_mul(
                        wrows[0:1, e * CW:(e + 1) * CW],
                        wrows[0:1, e * CW:(e + 1) * CW], row2[0:1, 512:1024]),
                        waits=[t_rcp], force=True,
                        inc="dve" if e == NE - 1 else None)
                t_prev2 = t_prev
                t_wall = None
                for e in range(NE):
                    tb = b.emit("pe", lambda t, e=e: t.matmul(
                        PSB[0:64, 0:512], ones_row_bf[0:1, 0:64],
                        wrows[0:1, e * CW:(e + 1) * CW], start=True, stop=True),
                        waits=[t_wn, t_prev2], inc="pe")
                    tb2 = b.emit("pe", lambda t, e=e: t.matmul(
                        PSB[64:128, 0:512], ones_row_bf[0:1, 0:64],
                        wrows[0:1, e * CW:(e + 1) * CW], start=True, stop=True),
                        inc="pe")
                    t_wall = b.emit("dve", lambda v, e=e: v.tensor_copy(
                        Wall[:, e * CW:(e + 1) * CW], PSB[:, 0:512]),
                        waits=[tb2, t_wall_pool_read], inc="dve")
                    t_prev2 = t_wall
                t_p1_war = t_wall
                if tc + 1 < NCH:
                    h2_stats(tc + 1)

            if ei + 1 < NE:
                t_w2_load[(tc, ei + 1)] = (
                    load_w2(ei + 1, 0, [t_h1_read[(tc, ei)]]),
                    load_w2(ei + 1, 1, []))
            elif tc + 1 < NCH:
                t_w2_load[(tc + 1, -1)] = (
                    load_w2(-1, 0, [t_h1_read[(tc, ei)]]),
                    load_w2(-1, 1, []))

        t_mix_sq = t_sqm
        hst["sq_war"] = t_stm
        t_rows3 = ln_rows(PSB[:, 512:1024], PSB[:, 0:512], t_stm, RO_O)
        t_R3, t_A23 = ln_bcast(RO_O, t_rows3, PSB[:, 512:1024],
                               PSB[:, 0:512])
        for dc in range(ND):
            ob = of_sb[dc % 2]
            b.emit("dve", lambda v, dc=dc: v.tensor_mul(
                tmp0, mix_sb[:, dc * CW:(dc + 1) * CW], PSB[:, 512:1024]),
                waits=[t_R3, t_A23, t_of_war[dc % 2]]
                if dc < 2 else [t_of_war[dc % 2]])
            b.emit("dve", lambda v: v.tensor_add(tmp1, tmp0, PSB[:, 0:512]))
            t_f = b.emit("dve", lambda v, dc=dc, ob=ob, yb=yb: v.tensor_add(
                ob[:], tmp1, yb[:, dc * CW:(dc + 1) * CW]), inc="dve")
            if dc == ND - 1:
                t_final[tc % 2] = t_f
                t_final_prev = t_f
                t_p1_war = t_f
            t_of_war[dc % 2] = b.emit(
                "sync", lambda s, tc=tc, dc=dc, ob=ob: s.dma_start(
                    yt.rearrange("(dc p) t -> p dc t", p=128)
                    [:, dc, tc * CW:(tc + 1) * CW], ob[:]),
                waits=[t_f], inc="hw")

    b.emit("sync", lambda s: s.wait_ge(b.sems["hw"][0], b.sems["hw"][1]))
    ffn.close()

    b.realize()
    b.es.close()
    import sys
    print(f"[kernel build] ops: " +
          ", ".join(f"{e}={len(b.ops[e])}" for e in ENGS), file=sys.stderr)
    return nc


_NC_CACHE = {}


def kernel(**inputs):
    x = np.asarray(inputs["L0_attn_x"], dtype=np.float32)

    def f(name):
        return np.ascontiguousarray(np.asarray(inputs[name], dtype=np.float32))

    common = {
        "wq": f("L0_attn_wq"), "wk": f("L0_attn_wk"), "wv": f("L0_attn_wv"),
        "wo": f("L0_attn_wo"),
        "bq": f("L0_attn_bq").reshape(1, D), "bk": f("L0_attn_bk").reshape(1, D),
        "bv": f("L0_attn_bv").reshape(1, D), "bo": f("L0_attn_bo").reshape(1, D),
        "rwt": f("L0_ffn_router_w"), "rbt": f("L0_ffn_router_b").reshape(1, NE),
        "sw1": f("L0_ffn_sw1"), "sb1": f("L0_ffn_sb1").reshape(1, HID),
        "sw2": f("L0_ffn_sw2"), "sb2": f("L0_ffn_sb2").reshape(1, D),
        "rw1": f("L0_ffn_rw1"), "rb1": f("L0_ffn_rb1").reshape(1, NE * HID),
        "rw2": f("L0_ffn_rw2"), "rb2": f("L0_ffn_rb2").reshape(1, NE * D),
    }
    if "nc" not in _NC_CACHE:
        _NC_CACHE["nc"] = build()
    nc = _NC_CACHE["nc"]
    in_maps = []
    for bi in range(8):
        m = dict(common)
        m["xt"] = np.ascontiguousarray(x[bi].T)
        in_maps.append(m)
    res = run_bass_kernel_spmd(nc, in_maps, core_ids=list(range(8)))
    out = np.stack([np.asarray(r["yt"]).T for r in res.results], axis=0)
    return np.ascontiguousarray(out.astype(np.float32))
